# version 10
# baseline (speedup 1.0000x reference)
import numpy as np
import ml_dtypes
from contextlib import ExitStack

import concourse.bass as bass
import concourse.mybir as mybir
from concourse.bass_utils import run_bass_kernel_spmd

F32 = mybir.dt.float32
BF16 = mybir.dt.bfloat16
AF = mybir.ActivationFunctionType
OP = mybir.AluOpType
AX = mybir.AxisListType.X

B, T, D = 4, 4096, 1024
HALF = T // 2
NH, DH = 16, 64
CH = 128
EPS = 1e-5
INV_D = 1.0 / D


class Sem:
    def __init__(self, es, nc, name):
        self.h = es.enter_context(nc.semaphore(name))
        self.n = 0

    def inc(self, inst, k=1):
        inst.then_inc(self.h, k)
        self.n += k
        return self.n


def r3(ap2d, c):
    return ap2d.rearrange("p (c q) -> p c q", c=c)




def _finish(nc, s_st, s_dve, s_sce, s_te, tap_aps, s_ld, xn2T=None, stats=None, xnT_dram=None):
    for e in [nc.sync, nc.scalar, nc.vector, nc.tensor, nc.gpsimd]:
        for s in [s_ld, s_te, s_dve, s_sce, s_st]:
            e.wait_ge(s.h, s.n)
    if xn2T is not None and "xn2T" in tap_aps:
        s_st.inc(nc.sync.dma_start(tap_aps["xn2T"][:, :], xn2T[:, :]), 16)
    if stats is not None and "stats" in tap_aps:
        s_st.inc(nc.sync.dma_start(tap_aps["stats"][:, 0:16], stats[0][:, :]), 16)
        s_st.inc(nc.sync.dma_start(tap_aps["stats"][:, 16:32], stats[1][:, :]), 16)
    if xnT_dram is not None and "xnT" in tap_aps:
        s_st.inc(nc.sync.dma_start(tap_aps["xnT"][:, :], xnT_dram[:, :]), 16)
    nc.sync.wait_ge(s_st.h, s_st.n)

def build_nc(taps=(), upto="full", p2_lim=16, p2_scan=2, hs=4):
    LV = {"p1": 1, "p0b": 2, "p2": 3, "p3a": 4, "p3b": 5, "full": 6}[upto]
    nc = bass.Bass("TRN2", target_bir_lowering=False)
    dt = nc.dram_tensor

    x_own = dt("x_own", [HALF, D], F32, kind="ExternalInput").ap()
    x_other = dt("x_other", [HALF, D], F32, kind="ExternalInput").ap()
    chain_scale = dt("chain_scale", [128, 1], F32, kind="ExternalInput").ap()
    wts = {}
    for nm in ["wq", "wk", "wv", "wg", "wo", "fq", "fk", "fv", "fp"]:
        wts[nm] = dt(nm + "_t", [D, D], F32, kind="ExternalInput").ap()
    wr_t = dt("wr_t", [D, 15], F32, kind="ExternalInput").ap()
    ws_t = dt("ws_t", [D, 2], F32, kind="ExternalInput").ap()
    bcols = {}
    for nm in ["bq", "bk", "bg", "fbq", "fbk"]:
        bcols[nm] = dt(nm + "_c", [128, 8], F32, kind="ExternalInput").ap()
    bv_att_row = dt("bv_att_row", [1, D], F32, kind="ExternalInput").ap()
    fbv_row = dt("fbv_row", [1, D], F32, kind="ExternalInput").ap()
    bo_row = dt("bo_row", [1, D], F32, kind="ExternalInput").ap()
    bp_row = dt("bp_row", [1, D], F32, kind="ExternalInput").ap()
    brs_row = dt("brs_row", [1, 17], F32, kind="ExternalInput").ap()
    gamma_c = dt("gamma_c", [128, 8], F32, kind="ExternalInput").ap()
    beta_c = dt("beta_c", [128, 8], F32, kind="ExternalInput").ap()
    al_t2 = dt("al_t2", [128, 8], F32, kind="ExternalInput").ap()
    qe_t2 = dt("qe_t2", [128, 8], F32, kind="ExternalInput").ap()
    temp_row = dt("temp_row", [1, 16], F32, kind="ExternalInput").ap()

    out = dt("out", [HALF, D], F32, kind="ExternalOutput").ap()
    tap_aps = {}
    for name, shape, dtp in [("xn2T", [128, 8 * T], BF16), ("xnT", [128, 8 * T], BF16),
                             ("stats", [128, 32], F32), ("ya", [HALF, D], F32),
                             ("patq1", [128, 512], F32), ("hT", [128, 8 * HALF], BF16),
                             ("fkT", [128, 8 * T], BF16), ("fvT", [128, 32 * 1024], BF16),
                             ("fqT", [128, 8 * HALF], BF16), ("negM", [128, 16], F32),
                             ("blkK", [128, 4096], BF16), ("blkV", [128, 2048], BF16),
                             ("blkQ", [128, 4096], BF16), ("blkG", [128, 4096], BF16),
                             ("og8", [128, 1024], BF16), ("Hf", [128, 512], F32),
                             ("qt8", [128, 1024], BF16), ("kt8", [128, 1024], BF16),
                             ("kbtok", [128, 1024], BF16), ("Am", [128, 512], BF16),
                             ("kbT", [128, 1024], BF16), ("patkb", [128, 1024], BF16),
                             ("xn2Tb", [128, 4096], BF16)]:
        if name in taps:
            tap_aps[name] = dt("tap_" + name, shape, dtp, kind="ExternalOutput").ap()

    xnT_dram = dt("xnT_s", [128, 8 * T], BF16, kind="Internal").ap()
    xn2T_dram = dt("xn2T_s", [128, 8 * T], BF16, kind="Internal").ap()
    ya_dram = dt("ya_s", [HALF, D], F32, kind="Internal").ap()
    hT_dram = dt("hT_s", [128, 8 * HALF], BF16, kind="Internal").ap()

    ident_np = np.eye(128, dtype=ml_dtypes.bfloat16)
    ones_np = np.ones((128, 128), dtype=ml_dtypes.bfloat16)
    onesf_np = np.ones((128, 128), dtype=np.float32)
    mask_np = np.triu(np.ones((128, 128), np.float32))
    r64 = np.arange(64, dtype=np.float32)
    s128 = np.arange(128, dtype=np.float32)
    E_q1 = np.broadcast_to(r64 + 1.0, (128, 64)).copy()
    E_k = np.broadcast_to(-(r64 + 1.0), (128, 64)).copy()
    E_kb = np.broadcast_to(127.0 - s128, (128, 128)).copy()

    es = ExitStack()
    with es:
        sb = lambda name, shape, dtp: es.enter_context(nc.sbuf_tensor(name, shape, dtp))
        es12 = ExitStack()
        sb12 = lambda name, shape, dtp: es12.enter_context(nc.sbuf_tensor(name, shape, dtp))
        ident = sb("ident", [128, 128], BF16)
        onesb = sb("onesb", [128, 128], BF16)
        onesf = sb("onesf", [128, 128], F32)
        maskb = sb("maskb", [128, 128], BF16)
        lna = sb("lna", [128, 8], F32)
        alC = sb("alC", [128, 8], F32)
        al64 = sb("al64", [128, 8], F32)
        invab = sb("invab", [128, 8], BF16)
        al65b = sb("al65b", [128, 8], BF16)
        qecol = sb("qecol", [128, 8], F32)
        csc = sb("csc", [128, 1], F32)
        gcol = sb("gcol", [128, 8], F32)
        bcol_f32 = {nm: sb(nm + "_f", [128, 8], F32) for nm in bcols}
        beta_bf = sb("beta_bf", [128, 8], BF16)
        fbvrow = sb("fbvrow", [1, D], BF16)
        borow = sb("borow", [1, D], BF16)
        bprow = sb("bprow", [1, D], BF16)
        brsrow = sb("brsrow", [1, 17], BF16)
        sp8 = sb("sp8", [1, 16], F32)
        spb = sb("spb", [128, 16], F32)
        spp = sb("spp", [128, 8], F32)
        negM = sb("negM", [128, 16], F32)
        negMp = sb("negMp", [128, 8], F32)
        stats_s1 = sb("stats_s1", [128, 16], F32)
        stats_s2 = sb("stats_s2", [128, 16], F32)
        bvarow_b = sb12("bvarow_b", [1, D], BF16)

        S2 = [es.enter_context(nc.psum_tensor(f"s2_{i}", [128, 1024], F32)) for i in range(2)]
        pb_single = [es.enter_context(nc.psum_tensor(f"pb{i}", [128, 512], F32)) for i in range(3)]
        pT = es.enter_context(nc.psum_tensor("pT", [128, 1024], BF16))
        pb = [S2[0].ap()[:, 0:512], S2[0].ap()[:, 512:1024],
              S2[1].ap()[:, 0:512], S2[1].ap()[:, 512:1024],
              pb_single[0].ap()[:, :], pb_single[1].ap()[:, :], pb_single[2].ap()[:, :]]

        s_ld = Sem(es, nc, "s_ld")
        s_ldw = Sem(es, nc, "s_ldw")
        s_te = Sem(es, nc, "s_te")
        s_dve = Sem(es, nc, "s_dve")
        s_sce = Sem(es, nc, "s_sce")
        s_st = Sem(es, nc, "s_st")
        ALL_SEMS = [s_ld, s_ldw, s_te, s_dve, s_sce, s_st]

        def barrier():
            for e in [nc.sync, nc.scalar, nc.vector, nc.tensor, nc.gpsimd]:
                for s in ALL_SEMS:
                    e.wait_ge(s.h, s.n)

        ic = nc.inline_tensor
        for dst, arr, cn in [(ident, ident_np, "c_id"), (onesb, ones_np, "c_on"),
                             (onesf, onesf_np, "c_onf"), (maskb, mask_np.astype(ml_dtypes.bfloat16), "c_mk"),
                             ]:
            s_ld.inc(nc.sync.dma_start(dst[:, :], ic(arr, cn).ap()[:, :]), 16)
        for nm in bcols:
            s_ld.inc(nc.sync.dma_start(bcol_f32[nm][:, :], bcols[nm][:, :]), 16)
        s_ld.inc(nc.sync.dma_start(gcol[:, :], gamma_c[:, :]), 16)
        s_ld.inc(nc.sync.dma_start(lna[:, :], al_t2[:, :]), 16)
        s_ld.inc(nc.sync.dma_start(csc[:, :], chain_scale[:, :]), 16)
        s_ld.inc(nc.sync.dma_start(sp8[:, :], temp_row[:, :]), 16)
        s_ldw.inc(nc.gpsimd.dma_start(beta_bf[:, :], beta_c[:, :]), 16)
        s_ld.inc(nc.sync.dma_start(qecol[:, :], qe_t2[:, :]), 16)
        s_ldw.inc(nc.gpsimd.dma_start(borow[:, :], bo_row[:, :]), 16)
        s_ldw.inc(nc.gpsimd.dma_start(bprow[:, :], bp_row[:, :]), 16)
        s_ldw.inc(nc.gpsimd.dma_start(brsrow[:, :], brs_row[:, :]), 16)
        s_ldw.inc(nc.gpsimd.dma_start(fbvrow[:, :], fbv_row[:, :]), 16)
        small_lds = s_ld.n
        small_ldw = s_ldw.n

        w_attn = {nm: sb12(nm + "_w", [128, 8 * D], BF16) for nm in ["wq", "wk", "wv", "wg", "wo"]}
        for nm in ["wq", "wk", "wv", "wg", "wo"]:
            s_ldw.inc(nc.gpsimd.dma_start(r3(w_attn[nm][:, :], 8),
                                          wts[nm][:, :].rearrange("(i p) o -> p i o", p=128)), 16)
        attn_w_ld = s_ldw.n

        nc.scalar.wait_ge(s_ld.h, small_lds)
        nc.scalar.wait_ge(s_ldw.h, small_ldw)
        v = s_sce.inc(nc.scalar.activation(alC[:, :], lna[:, :], AF.Exp, scale=-1.0))
        nc.vector.wait_ge(s_sce.h, v)
        v = s_dve.inc(nc.vector.tensor_scalar_add(alC[:, :], alC[:, :], 1.0))
        nc.scalar.wait_ge(s_dve.h, v)
        v = s_sce.inc(nc.scalar.activation(lna[:, :], alC[:, :], AF.Ln))
        nc.scalar.wait_ge(s_sce.h, v)
        v = s_sce.inc(nc.scalar.mul(lna[:, :], lna[:, :], -1.0))
        nc.scalar.wait_ge(s_sce.h, v)
        nc.scalar.activation(alC[:, :], lna[:, :], AF.Exp, scale=float(CH))
        nc.scalar.activation(al64[:, :], lna[:, :], AF.Exp, scale=64.0)
        nc.scalar.activation(invab[:, :], lna[:, :], AF.Exp, scale=-1.0)
        nc.scalar.activation(al65b[:, :], lna[:, :], AF.Exp, scale=65.0)
        v = s_sce.inc(nc.scalar.activation(sp8[:, :], sp8[:, :], AF.Exp))
        nc.vector.wait_ge(s_sce.h, v)
        v = s_dve.inc(nc.vector.tensor_scalar_add(sp8[:, :], sp8[:, :], 1.0))
        nc.scalar.wait_ge(s_dve.h, v)
        v = s_sce.inc(nc.scalar.activation(sp8[:, :], sp8[:, :], AF.Ln))
        nc.scalar.wait_ge(s_sce.h, v)
        sp8_done = s_sce.inc(nc.scalar.mul(sp8[:, :], sp8[:, :], 0.125))
        nc.tensor.wait_ge(s_sce.h, sp8_done)
        nc.tensor.wait_ge(s_ld.h, small_lds)
        v = s_te.inc(nc.tensor.matmul(pb[0][:, 0:16], onesf[0:1, 0:128], sp8[0:1, :],
                                      start=True, stop=True))
        nc.vector.wait_ge(s_te.h, v)
        v = s_dve.inc(nc.vector.tensor_copy(spb[:, :], pb[0][:, 0:16]))
        nc.vector.wait_ge(s_dve.h, v)
        nc.vector.tensor_copy(spp[0:64, :], spb[0:64, 0:16].rearrange("p (a b) -> p a b", b=2)[:, :, 0:1].squeeze(2))
        s_dve.inc(nc.vector.tensor_copy(spp[64:128, :], spb[64:128, 0:16].rearrange("p (a b) -> p a b", b=2)[:, :, 1:2].squeeze(2)))
        decay_sce = s_sce.n

        with ExitStack() as es1:
            sb1 = lambda name, shape, dtp: es1.enter_context(nc.sbuf_tensor(name, shape, dtp))
            xt2 = [sb1(f"xt2_{i}", [128, 2048], F32) for i in range(2)]
            xnf = sb1("xnf", [128, 2048], F32)
            xnb = sb1("xnb", [128, 2048], BF16)
            xn2b = sb1("xn2b", [128, 2048], BF16)
            stage = [sb1(f"xnTs{i}", [128, 2048], BF16) for i in range(2)]
            stage2 = [sb1(f"xn2Ts{i}", [128, 2048], BF16) for i in range(2)]
            cols = sb1("cols", [128, 24], F32)
            SX, SX2, MU, VAR, R1, SXN, SXN2, MU2, VAR2, R2, TMP = 0, 2, 4, 6, 8, 10, 12, 14, 16, 18, 20
            xfree_sce = {0: 0, 1: 0}
            xfree_dve = {0: 0, 1: 0}
            stfree = {0: 0, 1: 0}
            grp_store = {}

            def emit_p1_group(g):
                bi = g % 2
                t0 = 2 * g
                src_ = x_other if t0 < 16 else x_own
                o = (t0 if t0 < 16 else t0 - 16) * 128
                if g >= 2:
                    nc.sync.wait_ge(s_sce.h, xfree_sce[bi])
                    nc.sync.wait_ge(s_dve.h, xfree_dve[bi])
                ld_v = s_ld.inc(nc.sync.dma_start(
                    xt2[bi][:, :].rearrange("p (c d) -> p c d", c=2),
                    src_[o:o + 256, :].rearrange("(c p) d -> p c d", p=128)), 16)
                import os as _os3
                if _os3.environ.get("P1DBG") == "ldonly" and g >= 4:
                    xfree_sce[bi] = s_sce.n
                    xfree_dve[bi] = s_dve.n
                    grp_store[g] = s_st.n
                    return
                nc.scalar.wait_ge(s_ld.h, ld_v)
                for ti in range(2):
                    sq1 = s_sce.inc(nc.scalar.activation(xnf[:, ti * 1024:(ti + 1) * 1024],
                                                         xt2[bi][:, ti * 1024:(ti + 1) * 1024], AF.Square,
                                                         accum_out=cols[:, SX2 + ti:SX2 + ti + 1]))
                nc.vector.wait_ge(s_ld.h, ld_v)
                for ti in range(2):
                    rv_ = s_dve.inc(nc.vector.reduce_sum(cols[:, SX + ti:SX + ti + 1],
                                                         xt2[bi][:, ti * 1024:(ti + 1) * 1024], axis=AX))
                nc.vector.wait_ge(s_dve.h, rv_)
                nc.vector.wait_ge(s_sce.h, sq1)
                v_ = s_dve.inc(nc.vector.tensor_scalar_mul(cols[:, MU:MU + 2], cols[:, SX:SX + 2], INV_D))
                nc.vector.tensor_scalar_mul(cols[:, VAR:VAR + 2], cols[:, SX2:SX2 + 2], INV_D)
                nc.vector.wait_ge(s_dve.h, v_)
                v_ = s_dve.inc(nc.vector.tensor_tensor(cols[:, TMP:TMP + 2], cols[:, MU:MU + 2], cols[:, MU:MU + 2], OP.mult))
                nc.vector.wait_ge(s_dve.h, v_)
                v_ = s_dve.inc(nc.vector.tensor_tensor(cols[:, VAR:VAR + 2], cols[:, VAR:VAR + 2], cols[:, TMP:TMP + 2], OP.subtract))
                nc.vector.wait_ge(s_dve.h, v_)
                st1 = s_dve.inc(nc.vector.tensor_scalar_add(cols[:, VAR:VAR + 2], cols[:, VAR:VAR + 2], EPS))
                nc.scalar.wait_ge(s_dve.h, st1)
                v_ = s_sce.inc(nc.scalar.activation(cols[:, R1:R1 + 2], cols[:, VAR:VAR + 2], AF.Ln))
                nc.scalar.wait_ge(s_sce.h, v_)
                r1v = s_sce.inc(nc.scalar.activation(cols[:, R1:R1 + 2], cols[:, R1:R1 + 2], AF.Exp, scale=-0.5))
                nc.vector.wait_ge(s_sce.h, r1v)
                for ti in range(2):
                    xnfv = s_dve.inc(nc.vector.tensor_scalar(
                        xnf[:, ti * 1024:(ti + 1) * 1024], xt2[bi][:, ti * 1024:(ti + 1) * 1024],
                        cols[:, MU + ti:MU + ti + 1], cols[:, R1 + ti:R1 + ti + 1], OP.subtract, OP.mult))
                xfree_dve[bi] = xnfv
                xfree_sce[bi] = sq1
                nc.scalar.wait_ge(s_dve.h, xnfv)
                for ti in range(2):
                    sq2 = s_sce.inc(nc.scalar.activation(xnb[:, ti * 1024:(ti + 1) * 1024],
                                                         xnf[:, ti * 1024:(ti + 1) * 1024], AF.Square,
                                                         accum_out=cols[:, SXN2 + ti:SXN2 + ti + 1]))
                nc.vector.wait_ge(s_sce.h, sq2)
                for ti in range(2):
                    v_ = s_dve.inc(nc.vector.reduce_sum(cols[:, SXN + ti:SXN + ti + 1],
                                                        xnf[:, ti * 1024:(ti + 1) * 1024], axis=AX))
                nc.vector.wait_ge(s_dve.h, v_)
                v_ = s_dve.inc(nc.vector.tensor_scalar_mul(cols[:, MU2:MU2 + 2], cols[:, SXN:SXN + 2], INV_D))
                nc.vector.tensor_scalar_mul(cols[:, VAR2:VAR2 + 2], cols[:, SXN2:SXN2 + 2], INV_D)
                nc.vector.wait_ge(s_dve.h, v_)
                v_ = s_dve.inc(nc.vector.tensor_tensor(cols[:, TMP:TMP + 2], cols[:, MU2:MU2 + 2], cols[:, MU2:MU2 + 2], OP.mult))
                nc.vector.wait_ge(s_dve.h, v_)
                v_ = s_dve.inc(nc.vector.tensor_tensor(cols[:, VAR2:VAR2 + 2], cols[:, VAR2:VAR2 + 2], cols[:, TMP:TMP + 2], OP.subtract))
                nc.vector.wait_ge(s_dve.h, v_)
                st3 = s_dve.inc(nc.vector.tensor_scalar_add(cols[:, VAR2:VAR2 + 2], cols[:, VAR2:VAR2 + 2], EPS))
                nc.scalar.wait_ge(s_dve.h, st3)
                v_ = s_sce.inc(nc.scalar.activation(cols[:, R2:R2 + 2], cols[:, VAR2:VAR2 + 2], AF.Ln))
                nc.scalar.wait_ge(s_sce.h, v_)
                r2v = s_sce.inc(nc.scalar.activation(cols[:, R2:R2 + 2], cols[:, R2:R2 + 2], AF.Exp, scale=-0.5))
                nc.vector.wait_ge(s_sce.h, r2v)
                nc.vector.tensor_copy(xnb[:, :], xnf[:, :])
                for ti in range(2):
                    nc.vector.tensor_scalar(xn2b[:, ti * 1024:(ti + 1) * 1024], xnf[:, ti * 1024:(ti + 1) * 1024],
                                            cols[:, MU2 + ti:MU2 + ti + 1], cols[:, R2 + ti:R2 + ti + 1],
                                            OP.subtract, OP.mult)
                if t0 >= 16:
                    oc = t0 - 16
                    nc.vector.tensor_scalar_add(stats_s1[:, oc:oc + 2], cols[:, R1:R1 + 2], 1.0)
                    nc.vector.tensor_tensor(stats_s2[:, oc:oc + 2], cols[:, MU:MU + 2], cols[:, R1:R1 + 2], OP.mult)
                bufv = s_dve.inc(nc.vector.engine_nop())
                import os as _os2
                if _os2.environ.get("P1DBG") == "notrans" and g >= 4:
                    return
                nc.tensor.wait_ge(s_dve.h, bufv)
                nc.vector.wait_ge(s_st.h, stfree[bi])
                for rnd in range(2):
                    for c in range(8):
                        i1 = nc.tensor.transpose(pT[:, c * 128:(c + 1) * 128],
                                                 xnb[:, rnd * 1024 + c * 128: rnd * 1024 + (c + 1) * 128], ident[:, :])
                    te1 = s_te.inc(i1)
                    nc.vector.wait_ge(s_te.h, te1)
                    for c in range(8):
                        i2 = nc.vector.tensor_copy(stage[bi][:, rnd * 1024 + c * 128: rnd * 1024 + (c + 1) * 128],
                                                   pT[:, c * 128:(c + 1) * 128])
                    ev1 = s_dve.inc(i2)
                    nc.tensor.wait_ge(s_dve.h, ev1)
                nc.sync.wait_ge(s_dve.h, ev1)
                for ti in range(2):
                    sc_ = s_st.inc(nc.sync.dma_start(
                        r3(xnT_dram[:, :], 8)[:, :, (t0 + ti) * 128:(t0 + ti + 1) * 128],
                        r3(stage[bi][:, ti * 1024:(ti + 1) * 1024], 8)), 16)
                stfree[bi] = sc_
                for rnd in range(2):
                    for c in range(8):
                        i3 = nc.tensor.transpose(pT[:, c * 128:(c + 1) * 128],
                                                 xn2b[:, rnd * 1024 + c * 128: rnd * 1024 + (c + 1) * 128], ident[:, :])
                    te2 = s_te.inc(i3)
                    nc.vector.wait_ge(s_te.h, te2)
                    for c in range(8):
                        i4 = nc.vector.tensor_copy(stage2[bi][:, rnd * 1024 + c * 128: rnd * 1024 + (c + 1) * 128],
                                                   pT[:, c * 128:(c + 1) * 128])
                    ev2 = s_dve.inc(i4)
                    nc.tensor.wait_ge(s_dve.h, ev2)
                nc.sync.wait_ge(s_dve.h, ev2)
                for ti in range(2):
                    sc_ = s_st.inc(nc.sync.dma_start(
                        r3(xn2T_dram[:, :], 8)[:, :, (t0 + ti) * 128:(t0 + ti + 1) * 128],
                        r3(stage2[bi][:, ti * 1024:(ti + 1) * 1024], 8)), 16)
                stfree[bi] = sc_
                grp_store[g] = sc_

            emit_p1_group(0)
            emit_p1_group(1)

            if LV < 2:
                for name in list(tap_aps):
                    pass
                _finish(nc, s_st, s_dve, s_sce, s_te, tap_aps, s_ld, xn2T=xn2T, stats=(stats_s1, stats_s2), xnT_dram=xnT_dram)
                es12.close()
                return nc, tap_aps
            nc.tensor.wait_ge(s_ldw.h, attn_w_ld)
            biasf = {nm: sb1("bias_" + nm, [128, 8], F32) for nm in ["bq", "bk", "bg"]}
            bmfree = [0, 0]
            for wnm, bnm in [("wq", "bq"), ("wk", "bk"), ("wg", "bg")]:
                w = w_attn[wnm]
                for p in range(8):
                    bank = pb[p % 2]
                    nc.tensor.wait_ge(s_dve.h, bmfree[p % 2])
                    for i in range(8):
                        mi = nc.tensor.matmul(bank[:, p:p + 1], w[:, i * D + p * 128: i * D + (p + 1) * 128],
                                              beta_bf[:, i:i + 1], start=(i == 0), stop=(i == 7))
                    bv_ = s_te.inc(mi)
                    nc.vector.wait_ge(s_te.h, bv_)
                    bmfree[p % 2] = s_dve.inc(nc.vector.tensor_add(biasf[bnm][:, p:p + 1], bank[:, p:p + 1],
                                                                   bcol_f32[bnm][:, p:p + 1]))
            nc.tensor.wait_ge(s_dve.h, max(bmfree))
            es0b = ExitStack()
            bvarow = es0b.enter_context(nc.sbuf_tensor("bvarow", [1, D], F32))
            bvl = s_ld.inc(nc.sync.dma_start(bvarow[:, :], bv_att_row[:, :]), 16)
            nc.vector.wait_ge(s_ld.h, bvl)
            for hh in range(2):
                bank = pb[hh]
                for i in range(8):
                    mi = nc.tensor.matmul(bank[0:1, 0:512], beta_bf[:, i:i + 1],
                                          w_attn["wv"][:, i * D + hh * 512: i * D + hh * 512 + 512],
                                          start=(i == 0), stop=(i == 7))
                bv_ = s_te.inc(mi)
                nc.vector.wait_ge(s_te.h, bv_)
                fv_ = s_dve.inc(nc.vector.tensor_tensor(bvarow_b[0:1, hh * 512:(hh + 1) * 512],
                                                        bank[0:1, 0:512], bvarow[0:1, hh * 512:(hh + 1) * 512], OP.add))
            nc.tensor.wait_ge(s_dve.h, fv_)
            es0b.close()
            for wnm in ["wq", "wk", "wv", "wg"]:
                for i in range(8):
                    gi = nc.vector.tensor_scalar_mul(w_attn[wnm][:, i * D:(i + 1) * D],
                                                     w_attn[wnm][:, i * D:(i + 1) * D], gcol[:, i:i + 1])
            fold_done = s_dve.inc(gi)
            pat_q1 = sb1("pat_q1", [128, 512], BF16)
            pat_k = sb1("pat_k", [128, 512], BF16)
            pat_kb = sb1("pat_kb", [128, 1024], BF16)
            with ExitStack() as esp:
                patf = esp.enter_context(nc.sbuf_tensor("patf", [128, 2048], F32))
                e_q1 = esp.enter_context(nc.sbuf_tensor("e_q1", [128, 64], F32))
                e_k = esp.enter_context(nc.sbuf_tensor("e_k", [128, 64], F32))
                e_kb = esp.enter_context(nc.sbuf_tensor("e_kb", [128, 128], F32))
                ev_ = 0
                for dst_, arr_, cn_ in [(e_q1, E_q1, "c_eq1"), (e_k, E_k, "c_ek"), (e_kb, E_kb, "c_ekb")]:
                    ev_ = s_ld.inc(nc.sync.dma_start(dst_[:, :], nc.inline_tensor(arr_, cn_).ap()[:, :]), 16)
                nc.vector.wait_ge(s_ld.h, ev_)
                nc.vector.wait_ge(s_sce.h, decay_sce)
                for p in range(8):
                    nc.vector.tensor_scalar_mul(patf[:, p * 64:(p + 1) * 64], e_q1[:, :], lna[:, p:p + 1])
                    nc.vector.tensor_scalar_mul(patf[:, 512 + p * 64: 512 + (p + 1) * 64], e_k[:, :], lna[:, p:p + 1])
                    pi = nc.vector.tensor_scalar_mul(patf[:, 1024 + p * 128: 1024 + (p + 1) * 128], e_kb[:, :], lna[:, p:p + 1])
                pv = s_dve.inc(pi)
                nc.scalar.wait_ge(s_dve.h, pv)
                nc.scalar.activation(pat_q1[:, :], patf[:, 0:512], AF.Exp)
                nc.scalar.activation(pat_k[:, :], patf[:, 512:1024], AF.Exp)
                si = nc.scalar.activation(pat_kb[:, :], patf[:, 1024:2048], AF.Exp)
                pat_done = s_sce.inc(si)
                nc.vector.wait_ge(s_sce.h, pat_done)
            if "patq1" in tap_aps:
                nc.gpsimd.wait_ge(s_sce.h, pat_done)
                s_ldw.inc(nc.gpsimd.dma_start(tap_aps["patq1"][:, :], pat_q1[:, :]), 16)

            sb2 = sb1
            xn2Tb2 = [sb2(f"xn2Tb{i}", [128, 4096], BF16) for i in range(2)]
            blkK = sb2("blkK", [128, 4096], BF16)
            blkV = sb2("blkV", [128, 2048], BF16)
            blkQ = sb2("blkQ", [128, 4096], BF16)
            blkG = sb2("blkG", [128, 4096], BF16)
            kbT = sb2("kbT", [128, 1024], BF16)
            kbtok = sb2("kbtok", [128, 1024], BF16)
            qt8 = sb2("qt8", [128, 1024], BF16)
            qh8 = sb2("qh8", [128, 512], BF16)
            kt8 = sb2("kt8", [128, 1024], BF16)
            kh8 = sb2("kh8", [128, 512], BF16)
            Am = sb2("Am", [128, 512], BF16)
            og8 = sb2("og8", [128, 1024], BF16)
            Hf = sb2("Hf", [128, 512], F32)
            Hb = sb2("Hb", [128, 512], BF16)
            H64b = sb2("H64b", [128, 512], BF16)
            xtile = sb2("xtile", [128, D], F32)

            alC3 = alC[:, :].unsqueeze(2).broadcast_to([128, 8, 64])
            al643 = al64[:, :].unsqueeze(2).broadcast_to([128, 8, 64])
            inva3 = invab[:, :].unsqueeze(2).broadcast_to([128, 8, 64])
            al653 = al65b[:, :].unsqueeze(2).broadcast_to([128, 8, 64])
            mask2 = maskb[:, :].unsqueeze(1).broadcast_to([128, 2, 128])
            pat_q1_4 = pat_q1[:, :].rearrange("p (a c) -> p a c", a=8).unsqueeze(2).broadcast_to([128, 8, 2, 64])
            pat_k_4 = pat_k[:, :].rearrange("p (a c) -> p a c", a=8).unsqueeze(2).broadcast_to([128, 8, 2, 64])

            mset = s_dve.inc(nc.vector.memset(Hf[:, :], 0.0))
            nc.vector.memset(pb[2][:, :], 0.0)
            nc.vector.memset(pb[3][:, :], 0.0)
            mset = s_dve.inc(nc.vector.memset(pb[4][:, :], 0.0))

            pbfree = [0, 0]
            pbfree_s = [0, 0]
            yfree = 0
            ogfree = [0, 0]
            dfree = 0
            ptfree = 0
            gguard = 0
            xfree = 0
            ystfree = {0: 0, 1: 0}
            sfree = [0, 0]

            def proj_feat(wb, biascol, dst, xb, silu=False):
                for p in range(8):
                    bank = pb[p % 2]
                    nc.tensor.wait_ge(s_dve.h, pbfree[p % 2])
                    nc.tensor.wait_ge(s_sce.h, pbfree_s[p % 2])
                    for i in range(8):
                        mi = nc.tensor.matmul(bank[:, 0:512], wb[:, i * D + p * 128: i * D + (p + 1) * 128],
                                              xb[:, i * 512:(i + 1) * 512],
                                              start=(i == 0), stop=(i == 7))
                    tv = s_te.inc(mi)
                    dst3 = r3(dst[:, :], 4)[:, :, p * 128:(p + 1) * 128]
                    src3 = r3(bank[:, 0:512], 4)
                    if silu:
                        nc.scalar.wait_ge(s_te.h, tv)
                        nc.scalar.wait_ge(s_dve.h, gguard)
                        ev = s_sce.inc(nc.scalar.activation(dst3, src3, AF.Silu, bias=biascol[:, p:p + 1]))
                        pbfree_s[p % 2] = ev
                        sfree[p % 2] = ev
                    else:
                        nc.vector.wait_ge(s_te.h, tv)
                        ev = s_dve.inc(nc.vector.tensor_scalar(dst3, src3, biascol[:, p:p + 1], None, OP.add))
                        pbfree[p % 2] = ev

            def proj_tok(wb, brow, dst, xb, half):
                for tc in range(2 * half, 2 * half + 2):
                    for fh in range(2):
                        bk_ = (tc * 2 + fh) % 2
                        bank = pb[bk_]
                        nc.tensor.wait_ge(s_dve.h, pbfree[bk_])
                        nc.tensor.wait_ge(s_sce.h, pbfree_s[bk_])
                        for i in range(8):
                            nc.tensor.matmul(bank[:, 0:512],
                                             xb[:, i * 512 + tc * 128: i * 512 + (tc + 1) * 128],
                                             wb[:, i * D + fh * 512: i * D + fh * 512 + 512],
                                             start=(i == 0), stop=False)
                        mi = nc.tensor.matmul(bank[:, 0:512], onesb[0:1, 0:128],
                                              brow[0:1, fh * 512:fh * 512 + 512], start=False, stop=True)
                        tv = s_te.inc(mi)
                        nc.vector.wait_ge(s_te.h, tv)
                        ev = s_dve.inc(nc.vector.tensor_copy(dst[:, (tc % 2) * 1024 + fh * 512: (tc % 2) * 1024 + fh * 512 + 512],
                                                             bank[:, 0:512]))
                        pbfree[bk_] = ev

            x2g = {0: 0, 1: 0}

            def emit_p2_blk(b):
                nonlocal yfree, dfree, ptfree, gguard
                heavy = b >= 4
                base_lo = b * 512
                xn2Tb = xn2Tb2[b % 2]
                nc.sync.wait_ge(s_te.h, x2g[b % 2])
                nc.sync.wait_ge(s_st.h, grp_store[2 * b + 1])
                x2v = s_ld.inc(nc.sync.dma_start(
                    r3(xn2Tb[:, :], 8), r3(xn2T_dram[:, :], 8)[:, :, base_lo:base_lo + 512]), 16)
                nc.tensor.wait_ge(s_ld.h, x2v)
                proj_feat(w_attn["wk"], biasf["bk"], blkK, xn2Tb)
                proj_tok(w_attn["wv"], bvarow_b, blkV, xn2Tb, 0)
                if heavy:
                    proj_feat(w_attn["wq"], biasf["bq"], blkQ, xn2Tb)
                    proj_feat(w_attn["wg"], biasf["bg"], blkG, xn2Tb, silu=True)
                if heavy:
                    nc.vector.wait_ge(s_sce.h, max(sfree))
                for c in range(2 * p2_scan):
                    cab = 4 * b + c
                    if c == 2:
                        proj_tok(w_attn["wv"], bvarow_b, blkV, xn2Tb, 1)
                    if c == 2 * p2_scan - 1:
                        x2g[b % 2] = s_te.n
                    csl = slice(c * 1024, (c + 1) * 1024)
                    ame = 0
                    v1 = s_dve.inc(nc.vector.tensor_tensor(kbT[:, :], blkK[:, csl], pat_kb[:, :], OP.mult))
                    nc.tensor.wait_ge(s_dve.h, max(v1, ptfree))
                    for p in range(8):
                        ti = nc.tensor.transpose(pT[:, p * 128:(p + 1) * 128], kbT[:, p * 128:(p + 1) * 128], ident[:, :])
                    tv = s_te.inc(ti)
                    nc.vector.wait_ge(s_te.h, tv)
                    kbe = s_dve.inc(nc.vector.tensor_copy(kbtok[:, :], pT[:, :]))
                    ptfree = kbe
                    if heavy and hs >= 1:
                        nc.vector.wait_ge(s_sce.h, max(sfree))
                        nc.vector.tensor_tensor(r3(qt8[:, :], 8).rearrange("p a (b q) -> p a b q", b=2),
                                                r3(blkQ[:, csl], 8).rearrange("p a (b q) -> p a b q", b=2),
                                                pat_q1_4, OP.mult)
                        qt3 = r3(qt8[:, :], 8)
                        nc.vector.tensor_tensor(r3(qh8[:, :], 8), qt3[:, :, 64:128], inva3, OP.mult)
                        nc.vector.tensor_tensor(r3(kt8[:, :], 8).rearrange("p a (b q) -> p a b q", b=2),
                                                r3(blkK[:, csl], 8).rearrange("p a (b q) -> p a b q", b=2),
                                                pat_k_4, OP.mult)
                        kt3 = r3(kt8[:, :], 8)
                        nc.vector.tensor_tensor(r3(kh8[:, :], 8), kt3[:, :, 0:64], al653, OP.mult)
                        nc.vector.tensor_copy(Hb[:, :], Hf[:, :])
                        qk_done = s_dve.inc(nc.vector.tensor_tensor(H64b[:, :], Hf[:, :], al643, OP.mult))
                        ame = 0
                        for g in range(4 if hs >= 2 else 0):
                            p0, p1 = 2 * g, 2 * g + 1
                            nc.tensor.wait_ge(s_dve.h, max(qk_done, ame, dfree))
                            for gi_, p in enumerate((p0, p1)):
                                for hh, ro in ((0, 0), (1, 64)):
                                    bank = pb[2] if hh == 0 else pb[5]
                                    co = gi_ * 128
                                    ksl = slice(ro, ro + 64)
                                    nc.tensor.matmul(bank[0:64, co + 0:co + 64],
                                                     kt8[ksl, p * 128:p * 128 + 64], qt8[ksl, p * 128:p * 128 + 64],
                                                     start=True, stop=True, tile_position=(ro, 0))
                                    nc.tensor.matmul(bank[64:128, co + 64:co + 128],
                                                     kt8[ksl, p * 128 + 64:p * 128 + 128], qt8[ksl, p * 128 + 64:p * 128 + 128],
                                                     start=True, stop=True, tile_position=(ro, 64))
                                    am = nc.tensor.matmul(bank[0:64, co + 64:co + 128],
                                                          kh8[ksl, p * 64:p * 64 + 64], qh8[ksl, p * 64:p * 64 + 64],
                                                          start=True, stop=True, tile_position=(ro, 0))
                            atv = s_te.inc(am)
                            nc.vector.wait_ge(s_te.h, atv)
                            nc.vector.tensor_tensor(
                                r3(Am[:, 0:256], 2), r3(pb[2][:, 0:256], 2), mask2, OP.mult)
                            ame = s_dve.inc(nc.vector.tensor_tensor(
                                r3(Am[:, 256:512], 2), r3(pb[5][:, 0:256], 2), mask2, OP.mult))
                            for gi_, p in enumerate((p0, p1) if hs >= 3 else ()):
                                ob = pb[3] if p % 2 == 0 else pb[4]
                                abase = gi_ * 128
                                nc.tensor.wait_ge(s_dve.h, max(ame, ogfree[p % 2]))
                                nc.tensor.matmul(ob[0:64, 0:128], blkV[:, (c % 2) * 1024 + (2 * p) * 64: (c % 2) * 1024 + (2 * p) * 64 + 64],
                                                 Am[:, abase:abase + 128], start=True, stop=False, tile_position=(0, 0))
                                nc.tensor.matmul(ob[0:64, 0:64], Hb[0:64, p * 64:p * 64 + 64],
                                                 qt8[0:64, p * 128:p * 128 + 64], start=False, stop=False, tile_position=(0, 0))
                                nc.tensor.matmul(ob[0:64, 64:128], H64b[0:64, p * 64:p * 64 + 64],
                                                 qt8[0:64, p * 128 + 64:p * 128 + 128], start=False, stop=False, tile_position=(0, 0))
                                nc.tensor.matmul(ob[64:128, 0:128], blkV[:, (c % 2) * 1024 + (2 * p + 1) * 64: (c % 2) * 1024 + (2 * p + 1) * 64 + 64],
                                                 Am[:, 256 + abase:256 + abase + 128], start=True, stop=False, tile_position=(0, 64))
                                nc.tensor.matmul(ob[64:128, 0:64], Hb[64:128, p * 64:p * 64 + 64],
                                                 qt8[64:128, p * 128:p * 128 + 64], start=False, stop=False, tile_position=(64, 64))
                                oi = nc.tensor.matmul(ob[64:128, 64:128], H64b[64:128, p * 64:p * 64 + 64],
                                                      qt8[64:128, p * 128 + 64:p * 128 + 128], start=False, stop=True, tile_position=(64, 64))
                                otv = s_te.inc(oi)
                                nc.vector.wait_ge(s_te.h, otv)
                                nc.vector.wait_ge(s_sce.h, max(sfree))
                                ogv = s_dve.inc(nc.vector.tensor_tensor(
                                    og8[:, p * 128:(p + 1) * 128], ob[:, 0:128],
                                    blkG[:, c * 1024 + p * 128: c * 1024 + (p + 1) * 128], OP.mult))
                                ogfree[p % 2] = ogv
                        gguard = s_dve.n
                    nc.tensor.wait_ge(s_dve.h, max(kbe, dfree, ame))
                    for p in range(8):
                        nc.tensor.matmul(pb[5][0:64, p * 64:(p + 1) * 64], kbtok[:, p * 128:p * 128 + 64],
                                         blkV[:, (c % 2) * 1024 + (2 * p) * 64: (c % 2) * 1024 + (2 * p) * 64 + 64],
                                         start=True, stop=True, tile_position=(0, 0))
                        di = nc.tensor.matmul(pb[5][64:128, p * 64:(p + 1) * 64], kbtok[:, p * 128 + 64:p * 128 + 128],
                                              blkV[:, (c % 2) * 1024 + (2 * p + 1) * 64: (c % 2) * 1024 + (2 * p + 1) * 64 + 64],
                                              start=True, stop=True, tile_position=(0, 64))
                    dtv = s_te.inc(di)
                    nc.vector.wait_ge(s_te.h, dtv)
                    nc.vector.tensor_tensor(r3(Hf[:, :], 8), r3(Hf[:, :], 8), alC3, OP.mult)
                    dfree = s_dve.inc(nc.vector.tensor_tensor(Hf[:, :], Hf[:, :], pb[5][:, :], OP.add))
                    if heavy and hs >= 4:
                        oc = 4 * (b - 4) + c
                        nc.sync.wait_ge(s_st.h, max(ystfree.values()) if isinstance(ystfree, dict) else max(ystfree))
                        xl = s_ld.inc(nc.sync.dma_start(xtile[:, :], x_own[oc * 128:(oc + 1) * 128, :]), 16)
                        for hh in range(2):
                            nc.tensor.wait_ge(s_dve.h, max(yfree, gguard))
                            for p in range(8):
                                nc.tensor.matmul(pb[6][:, :], og8[:, p * 128:(p + 1) * 128],
                                                 w_attn["wo"][:, p * D + hh * 512: p * D + hh * 512 + 512],
                                                 start=(p == 0), stop=False)
                            yi = nc.tensor.matmul(pb[6][:, :], onesb[0:1, 0:128], borow[0:1, hh * 512:hh * 512 + 512],
                                                  start=False, stop=True)
                            ytv = s_te.inc(yi)
                            nc.vector.wait_ge(s_te.h, ytv)
                            nc.vector.wait_ge(s_ld.h, xl)
                            nc.vector.tensor_scalar(xtile[:, hh * 512:(hh + 1) * 512], xtile[:, hh * 512:(hh + 1) * 512],
                                                    stats_s1[:, oc:oc + 1], stats_s2[:, oc:oc + 1], OP.mult, OP.subtract)
                            yev = s_dve.inc(nc.vector.tensor_tensor(xtile[:, hh * 512:(hh + 1) * 512],
                                                                    xtile[:, hh * 512:(hh + 1) * 512],
                                                                    pb[6][:, :], OP.add))
                            yfree = yev
                            nc.sync.wait_ge(s_dve.h, yev)
                            ystfree[hh] = s_st.inc(nc.sync.dma_start(
                                ya_dram[oc * 128:(oc + 1) * 128, hh * 512:(hh + 1) * 512],
                                xtile[:, hh * 512:(hh + 1) * 512]), 16)
                if b == 3:
                    nc.vector.wait_ge(s_dve.h, s_dve.n)
                    s_dve.inc(nc.vector.tensor_scalar_mul(Hf[:, :], Hf[:, :], csc[:, 0:1]))

            nc.tensor.wait_ge(s_dve.h, fold_done)
            nc.vector.wait_ge(s_sce.h, pat_done)
            import os as _os
            mode = _os.environ.get("ILMODE", "pre")
            if mode == "no":
                for g in range(2, 16):
                    emit_p1_group(g)
                for b in range(p2_lim // 2):
                    emit_p2_blk(b)
            elif mode == "post":
                emit_p1_group(2)
                emit_p1_group(3)
                for b in range(p2_lim // 2):
                    emit_p2_blk(b)
                    for g in (2 * b + 4, 2 * b + 5):
                        if g < 16:
                            emit_p1_group(g)
            else:
                for b in range(p2_lim // 2):
                    if 2 * b + 2 < 16:
                        emit_p1_group(2 * b + 2)
                        emit_p1_group(2 * b + 3)
                    emit_p2_blk(b)
            for nm, buf in [("blkK", blkK), ("blkV", blkV), ("blkQ", blkQ), ("blkG", blkG),
                            ("og8", og8), ("Hf", Hf), ("qt8", qt8), ("kt8", kt8),
                            ("kbtok", kbtok), ("Am", Am[:, 0:512]), ("kbT", kbT), ("patkb", pat_kb), ("xn2Tb", xn2Tb2[1])]:
                if nm in tap_aps:
                    barrier()
                    s_st.inc(nc.sync.dma_start(tap_aps[nm][:, :], buf if nm in ("Am",) else buf[:, :]), 16)
            nc.sync.wait_ge(s_st.h, s_st.n)

        if "ya" in tap_aps:
            barrier()
            s_st.inc(nc.sync.dma_start(tap_aps["ya"][:, :], ya_dram[:, :]), 16)
        if "xn2T" in tap_aps:
            barrier()
            s_st.inc(nc.sync.dma_start(tap_aps["xn2T"][:, :], xn2T_dram[:, :]), 16)
        nc.sync.wait_ge(s_st.h, s_st.n)
        barrier()
        es12.close()
        if LV < 4:
            _finish(nc, s_st, s_dve, s_sce, s_te, tap_aps, s_ld)
            return nc, tap_aps

        barrier()
        es3 = ExitStack()
        with es3:
            sb3 = lambda name, shape, dtp: es3.enter_context(nc.sbuf_tensor(name, shape, dtp))
            es3kv = ExitStack()
            sb3kv = lambda name, shape, dtp: es3kv.enter_context(nc.sbuf_tensor(name, shape, dtp))
            fkT = sb3kv("fkT", [128, 8 * T], BF16)
            fvT = sb3kv("fvT", [128, 32 * 1024], BF16)
            fqT = sb3kv("fqT", [128, 8 * HALF], BF16)
            with ExitStack() as es3a:
                wbuf = es3a.enter_context(nc.sbuf_tensor("wbuf", [128, 8 * D], BF16))
                xnTb2 = [es3a.enter_context(nc.sbuf_tensor(f"xnTb{i}", [128, 8 * 512], BF16)) for i in range(2)]
                pfree = [0, 0]
                wfree = s_te.n
                xg = {0: 0, 1: 0}
                for wnm, kind in [("fk", "k"), ("fv", "v"), ("fq", "q")]:
                    nc.gpsimd.wait_ge(s_te.h, wfree)
                    wl = s_ldw.inc(nc.gpsimd.dma_start(r3(wbuf[:, :], 8),
                                                       wts[wnm][:, :].rearrange("(i p) o -> p i o", p=128)), 16)
                    blks = range(8) if kind in ("k", "v") else range(4, 8)
                    for bix, blk in enumerate(blks):
                        xnTb = xnTb2[bix % 2]
                        nc.sync.wait_ge(s_te.h, xg[bix % 2])
                        xv = s_ld.inc(nc.sync.dma_start(
                            r3(xnTb[:, :], 8), r3(xnT_dram[:, :], 8)[:, :, blk * 512:(blk + 1) * 512]), 16)
                        nc.tensor.wait_ge(s_ld.h, xv)
                        nc.tensor.wait_ge(s_ldw.h, wl)
                        if kind in ("k", "q"):
                            bc = bcol_f32["fbk" if kind == "k" else "fbq"]
                            for p in range(8):
                                bank = pb[p % 2]
                                nc.tensor.wait_ge(s_dve.h, pfree[p % 2])
                                for i in range(8):
                                    mi = nc.tensor.matmul(bank[:, 0:512],
                                                          wbuf[:, i * D + p * 128: i * D + (p + 1) * 128],
                                                          xnTb[:, i * 512:(i + 1) * 512], start=(i == 0), stop=(i == 7))
                                tv = s_te.inc(mi)
                                nc.vector.wait_ge(s_te.h, tv)
                                if kind == "k":
                                    dst = fkT[:, p * T + blk * 512: p * T + (blk + 1) * 512]
                                else:
                                    dst = fqT[:, p * HALF + (blk - 4) * 512: p * HALF + (blk - 3) * 512]
                                pfree[p % 2] = s_dve.inc(nc.vector.tensor_scalar(dst, bank[:, 0:512],
                                                                                 bc[:, p:p + 1], None, OP.add))
                            xg[bix % 2] = s_te.n
                        else:
                            for tc in range(4):
                                for fh in range(2):
                                    bk_ = (tc * 2 + fh) % 2
                                    bank = pb[bk_]
                                    nc.tensor.wait_ge(s_dve.h, pfree[bk_])
                                    for i in range(8):
                                        nc.tensor.matmul(bank[:, 0:512],
                                                         xnTb[:, i * 512 + tc * 128: i * 512 + (tc + 1) * 128],
                                                         wbuf[:, i * D + fh * 512: i * D + fh * 512 + 512],
                                                         start=(i == 0), stop=False)
                                    mi = nc.tensor.matmul(bank[:, 0:512], onesb[0:1, 0:128],
                                                          fbvrow[0:1, fh * 512:fh * 512 + 512], start=False, stop=True)
                                    tv = s_te.inc(mi)
                                    nc.vector.wait_ge(s_te.h, tv)
                                    pfree[bk_] = s_dve.inc(nc.vector.tensor_copy(
                                        fvT[:, (blk * 4 + tc) * 1024 + fh * 512: (blk * 4 + tc) * 1024 + fh * 512 + 512],
                                        bank[:, 0:512]))
                        xg[bix % 2] = s_te.n
                    wfree = s_te.n
                nc.gpsimd.wait_ge(s_dve.h, s_dve.n)

            if "fkT" in tap_aps:
                barrier()
                s_st.inc(nc.sync.dma_start(tap_aps["fkT"][:, :], fkT[:, :]), 16)
            if "fvT" in tap_aps:
                barrier()
                s_st.inc(nc.sync.dma_start(tap_aps["fvT"][:, :], fvT[:, :]), 16)
            if "fqT" in tap_aps:
                barrier()
                s_st.inc(nc.sync.dma_start(tap_aps["fqT"][:, :], fqT[:, :]), 16)

            if LV < 5:
                barrier()
                es3kv.close()
                _finish(nc, s_st, s_dve, s_sce, s_te, tap_aps, s_ld)
                return nc, tap_aps
            barrier()
            with ExitStack() as es3b:
                sb3b = lambda name, shape, dtp: es3b.enter_context(nc.sbuf_tensor(name, shape, dtp))
                qs = sb3b("qs", [128, 512], BF16)
                qsq = sb3b("qsq", [128, 512], BF16)
                sr = sb3b("sr", [128, 512], F32)
                pt = [sb3b(f"pt{i}", [128, 1024], BF16) for i in range(3)]
                kmx = sb3b("kmx", [128, 8], F32)
                hst = [sb3b(f"hst{i}", [128, 512], BF16) for i in range(2)]
                hstfree = [0, 0]
                hsti = 0
                qesq = sb3b("qesq", [128, 8], F32)
                rowb = sb3b("rowb", [1, 32], F32)
                bcst = sb3b("bcst", [128, 512], BF16)

                for p in range(8):
                    kv = s_dve.inc(nc.vector.reduce_max(kmx[:, p:p + 1], fkT[:, p * T:(p + 1) * T], axis=AX,
                                                        apply_absolute_value=True))
                nc.vector.wait_ge(s_dve.h, kv)
                kv = s_dve.inc(nc.vector.tensor_tensor(kmx[:, :], kmx[:, :], kmx[:, :], OP.mult))
                qev = s_dve.inc(nc.vector.tensor_tensor(qesq[:, :], qecol[:, :], qecol[:, :], OP.mult))
                nc.tensor.wait_ge(s_dve.h, max(kv, qev))
                for hh in range(2):
                    rs_ = slice(64 * hh, 64 * hh + 64)
                    for p in range(8):
                        h = 2 * p + hh
                        nc.tensor.matmul(pb[0][0:1, h:h + 1], onesf[rs_, 0:1], kmx[rs_, p:p + 1],
                                         start=True, stop=True, tile_position=(64 * hh, 0))
                        mi = nc.tensor.matmul(pb[0][0:1, 16 + h:16 + h + 1], onesf[rs_, 0:1], qesq[rs_, p:p + 1],
                                              start=True, stop=True, tile_position=(64 * hh, 0))
                    sv_ = s_te.inc(mi)
                    nc.tensor.wait_ge(s_te.h, sv_)
                bv_ = sv_
                nc.scalar.wait_ge(s_te.h, bv_)
                v_ = s_sce.inc(nc.scalar.activation(pb[0][0:1, 0:32], pb[0][0:1, 0:32], AF.Ln))
                nc.scalar.wait_ge(s_sce.h, v_)
                v_ = s_sce.inc(nc.scalar.activation(pb[0][0:1, 0:32], pb[0][0:1, 0:32], AF.Exp, scale=0.5))
                nc.vector.wait_ge(s_sce.h, v_)
                v_ = s_dve.inc(nc.vector.tensor_copy(rowb[0:1, 0:32], pb[0][0:1, 0:32]))
                nc.vector.wait_ge(s_dve.h, v_)
                v_ = s_dve.inc(nc.vector.tensor_scalar_add(rowb[0:1, 16:32], rowb[0:1, 16:32], 1.0))
                nc.vector.wait_ge(s_dve.h, v_)
                v_ = s_dve.inc(nc.vector.tensor_tensor(rowb[0:1, 0:16], rowb[0:1, 0:16], rowb[0:1, 16:32], OP.mult))
                nc.vector.wait_ge(s_dve.h, v_)
                v_ = s_dve.inc(nc.vector.tensor_tensor(rowb[0:1, 0:16], rowb[0:1, 0:16], sp8[0:1, :], OP.mult))
                nc.vector.wait_ge(s_dve.h, v_)
                v_ = s_dve.inc(nc.vector.tensor_scalar_mul(rowb[0:1, 0:16], rowb[0:1, 0:16], -1.0))
                nc.tensor.wait_ge(s_dve.h, v_)
                mi = nc.tensor.matmul(pb[1][:, 0:16], onesf[0:1, 0:128], rowb[0:1, 0:16], start=True, stop=True)
                bv_ = s_te.inc(mi)
                nc.vector.wait_ge(s_te.h, bv_)
                v_ = s_dve.inc(nc.vector.tensor_copy(negM[:, :], pb[1][:, 0:16]))
                nc.vector.wait_ge(s_dve.h, v_)
                nm_v = s_dve.inc(nc.vector.tensor_tensor(
                    negMp[:, :], negM[:, 0:16].rearrange("p (a b) -> p a b", b=2)[:, :, 0:1].squeeze(2),
                    negM[:, 0:16].rearrange("p (a b) -> p a b", b=2)[:, :, 1:2].squeeze(2), OP.min))
                if "negM" in tap_aps:
                    nc.sync.wait_ge(s_dve.h, nm_v)
                    s_st.inc(nc.sync.dma_start(tap_aps["negM"][:, :], negM[:, :]), 16)
                nc.tensor.wait_ge(s_dve.h, nm_v)

                ppb = [0, 0]
                nqv = 0
                for p in range(8):
                    for qb in range(4):
                        qsl = slice(p * HALF + qb * 512, p * HALF + (qb + 1) * 512)
                        bk_ = (p * 4 + qb) % 2
                        nc.vector.wait_ge(s_te.h, nqv)
                        qv = s_dve.inc(nc.vector.tensor_tensor(qsq[:, :], fqT[:, qsl], fqT[:, qsl], OP.mult))
                        nc.tensor.wait_ge(s_dve.h, max(qv, ppb[bk_]))
                        nc.tensor.matmul(pb[3 * bk_][0:1, :], onesb[0:64, 0:1], qsq[0:64, :],
                                         start=True, stop=True, tile_position=(0, 0))
                        nv = s_te.inc(nc.tensor.matmul(pb[3 * bk_][64:65, :], onesb[64:128, 0:1], qsq[64:128, :],
                                                       start=True, stop=True, tile_position=(64, 64)))
                        nqv = nv
                        nc.scalar.wait_ge(s_te.h, nv)
                        for ro in (0, 64):
                            v_ = s_sce.inc(nc.scalar.activation(sr[ro:ro + 1, :], pb[3 * bk_][ro:ro + 1, :], AF.Ln))
                            nc.scalar.wait_ge(s_sce.h, v_)
                            v_ = s_sce.inc(nc.scalar.activation(sr[ro:ro + 1, :], sr[ro:ro + 1, :], AF.Exp, scale=-0.5))
                        rsv = v_
                        nc.tensor.wait_ge(s_sce.h, rsv)
                        nc.tensor.matmul(pb[1 + 3 * bk_][0:64, :], onesf[0:1, 0:64], sr[0:1, :], start=True, stop=True,
                                         tile_position=(0, 0))
                        bc = s_te.inc(nc.tensor.matmul(pb[1 + 3 * bk_][64:128, :], onesf[64:65, 0:64], sr[64:65, :],
                                                       start=True, stop=True, tile_position=(64, 64)))
                        nc.vector.wait_ge(s_te.h, bc)
                        v_ = s_dve.inc(nc.vector.tensor_tensor(qs[:, :], fqT[:, qsl], pb[1 + 3 * bk_][:, :], OP.mult))
                        nc.vector.wait_ge(s_dve.h, v_)
                        v_ = s_dve.inc(nc.vector.tensor_scalar(qs[:, :], qs[:, :], qecol[:, p:p + 1], None, OP.add))
                        nc.vector.wait_ge(s_dve.h, v_)
                        ppb[bk_] = s_dve.inc(nc.vector.tensor_scalar_mul(fqT[:, qsl], qs[:, :], spp[:, p:p + 1]))
                nc.tensor.wait_ge(s_dve.h, max(ppb))

                exp_done = {0: s_sce.n, 1: s_sce.n}
                avdone = {0: s_te.n, 1: s_te.n, 2: s_te.n}
                sv = {0: 0, 1: 0}
                hvfree = 0
                bcstv = 0
                for p in range(8):
                    for qb in range(4):
                        qsl = slice(p * HALF + qb * 512, p * HALF + (qb + 1) * 512)
                        for kc in range(33):
                            par = kc % 2
                            if kc < 32:
                                nc.tensor.wait_ge(s_sce.h, exp_done[par])
                                ksl = slice(p * T + kc * 128, p * T + (kc + 1) * 128)
                                nc.tensor.matmul(pb[2 * par][:, :], fkT[0:64, ksl], fqT[0:64, qsl], start=True, stop=True,
                                                 tile_position=(0, 0))
                                sv[par] = s_te.inc(nc.tensor.matmul(pb[2 * par + 1][:, :], fkT[64:128, ksl], fqT[64:128, qsl],
                                                                    start=True, stop=True, tile_position=(64, 0)))
                            if kc >= 1:
                                q_ = kc - 1
                                parq = q_ % 2
                                pt3 = q_ % 3
                                nc.scalar.wait_ge(s_te.h, sv[parq])
                                nc.scalar.wait_ge(s_te.h, avdone[pt3])
                                exp_done[parq] = s_sce.inc(nc.scalar.activation(
                                    pt[pt3][:, :], S2[parq].ap()[:, :], AF.Exp, bias=negMp[:, p:p + 1]))
                                nc.tensor.wait_ge(s_sce.h, exp_done[parq])
                                if q_ == 0:
                                    nc.tensor.wait_ge(s_dve.h, max(hvfree, bcstv))
                                vsl0 = slice(q_ * 1024 + (2 * p) * 64, q_ * 1024 + (2 * p) * 64 + 64)
                                vsl1 = slice(q_ * 1024 + (2 * p + 1) * 64, q_ * 1024 + (2 * p + 1) * 64 + 64)
                                nc.tensor.matmul(pb[4][0:64, :], fvT[:, vsl0], pt[pt3][:, 0:512],
                                                 start=(q_ == 0), stop=(q_ == 31), tile_position=(0, 0))
                                nc.tensor.matmul(pb[5][64:128, :], fvT[:, vsl1], pt[pt3][:, 512:1024],
                                                 start=(q_ == 0), stop=(q_ == 31), tile_position=(0, 64))
                                nc.tensor.matmul(pb[6][0:1, :], onesb[:, 0:1], pt[pt3][:, 0:512],
                                                 start=(q_ == 0), stop=(q_ == 31), tile_position=(0, 0))
                                av = nc.tensor.matmul(pb[6][64:65, :], onesb[:, 0:1], pt[pt3][:, 512:1024],
                                                      start=(q_ == 0), stop=(q_ == 31), tile_position=(0, 64))
                                avdone[pt3] = s_te.inc(av)
                        nc.vector.wait_ge(s_te.h, max(avdone.values()))
                        v_ = s_dve.inc(nc.vector.reciprocal(sr[0:1, :], pb[6][0:1, :]))
                        rv = s_dve.inc(nc.vector.reciprocal(sr[64:65, :], pb[6][64:65, :]))
                        nc.tensor.wait_ge(s_dve.h, rv)
                        nc.tensor.matmul(pb[6][0:64, :], onesf[0:1, 0:64], sr[0:1, :], start=True, stop=True,
                                         tile_position=(0, 0))
                        bc = s_te.inc(nc.tensor.matmul(pb[6][64:128, :], onesf[64:65, 0:64], sr[64:65, :],
                                                       start=True, stop=True, tile_position=(64, 64)))
                        nc.vector.wait_ge(s_te.h, bc)
                        nc.vector.wait_ge(s_st.h, hstfree[hsti])
                        bcstv = s_dve.inc(nc.vector.tensor_copy(bcst[:, :], pb[6][:, :]))
                        nc.vector.tensor_tensor(hst[hsti][0:64, :], pb[4][0:64, :], bcst[0:64, :], OP.mult)
                        hvfree = s_dve.inc(nc.vector.tensor_tensor(hst[hsti][64:128, :], pb[5][64:128, :],
                                                                   bcst[64:128, :], OP.mult))
                        nc.sync.wait_ge(s_dve.h, hvfree)
                        hstfree[hsti] = s_st.inc(nc.sync.dma_start(hT_dram[:, qsl], hst[hsti][:, :]), 16)
                        hsti ^= 1

            barrier()
            es3kv.close()
            if LV < 6:
                if "hT" in tap_aps and LV >= 5:
                    s_st.inc(nc.sync.dma_start(tap_aps["hT"][:, :], hT_dram[:, :]), 16)
                _finish(nc, s_st, s_dve, s_sce, s_te, tap_aps, s_ld)
                return nc, tap_aps
            if "hT" in tap_aps:
                s_st.inc(nc.sync.dma_start(tap_aps["hT"][:, :], hT_dram[:, :]), 16)
                nc.sync.wait_ge(s_st.h, s_st.n)

            barrier()
            with ExitStack() as es3c:
                sb3c = lambda name, shape, dtp: es3c.enter_context(nc.sbuf_tensor(name, shape, dtp))
                fpw = sb3c("fpw", [128, 8 * D], BF16)
                hTt = [sb3c(f"hTt{i}", [128, 1024], BF16) for i in range(2)]
                hTtfree = [0, 0]
                wrs = sb3c("wrs", [128, 8 * 17], BF16)
                lg = sb3c("lg", [128, 24], F32)
                m8 = sb3c("m8", [128, 8], F32)
                gcolb = sb3c("gcolb", [128, 1], BF16)
                grow = sb3c("grow", [1, 128], BF16)
                hg8 = sb3c("hg8", [128, 1024], BF16)
                yac = [sb3c(f"yac{i}", [128, 512], F32) for i in range(2)]
                outb = [sb3c(f"outb{i}", [128, 512], F32) for i in range(2)]
                MX15, S15, RC15, T3, MX2, SW0, SW1, GTE = 15, 16, 17, 18, 19, 20, 21, 22

                wv1 = s_ldw.inc(nc.gpsimd.dma_start(r3(fpw[:, :], 8),
                                                    wts["fp"][:, :].rearrange("(i p) o -> p i o", p=128)), 16)
                wv2 = s_ldw.inc(nc.gpsimd.dma_start(
                    r3(wrs[:, :], 8)[:, :, 0:15], wr_t[:, :].rearrange("(i p) o -> p i o", p=128)), 16)
                wv2 = s_ldw.inc(nc.gpsimd.dma_start(
                    r3(wrs[:, :], 8)[:, :, 15:17], ws_t[:, :].rearrange("(i p) o -> p i o", p=128)), 16)
                yfree2 = 0
                pfree2 = 0
                yacfree = [0, 0]
                outfree = [0, 0]
                grfree = 0
                for tc in range(16):
                    ht = hTt[tc % 2]
                    nc.sync.wait_ge(s_dve.h, hTtfree[tc % 2])
                    hl = s_ld.inc(nc.sync.dma_start(
                        r3(ht[:, :], 8), r3(hT_dram[:, :], 8)[:, :, tc * 128:(tc + 1) * 128]), 16)
                    nc.tensor.wait_ge(s_ldw.h, wv2)
                    nc.tensor.wait_ge(s_dve.h, pfree2)
                    nc.tensor.wait_ge(s_ld.h, hl)
                    for i in range(8):
                        nc.tensor.matmul(pb[0][:, 0:17], ht[:, i * 128:(i + 1) * 128],
                                         wrs[:, i * 17:(i + 1) * 17], start=(i == 0), stop=False)
                    li = nc.tensor.matmul(pb[0][:, 0:17], onesb[0:1, 0:128], brsrow[0:1, :], start=False, stop=True)
                    lv = s_te.inc(li)
                    nc.vector.wait_ge(s_te.h, lv)
                    pfree2 = s_dve.inc(nc.vector.tensor_copy(lg[:, 0:17], pb[0][:, 0:17]))
                    nc.vector.wait_ge(s_dve.h, pfree2)
                    v_ = s_dve.inc(nc.vector.reduce_max(lg[:, MX15:MX15 + 1], lg[:, 0:15], axis=AX, negate=True))
                    nc.vector.wait_ge(s_dve.h, v_)
                    v2_ = s_dve.inc(nc.vector.reduce_max(lg[:, MX2:MX2 + 1], lg[:, 15:17], axis=AX, negate=True))
                    nc.scalar.wait_ge(s_dve.h, v2_)
                    e_ = s_sce.inc(nc.scalar.activation(lg[:, 0:15], lg[:, 0:15], AF.Exp, bias=lg[:, MX15:MX15 + 1]))
                    e2_ = s_sce.inc(nc.scalar.activation(lg[:, 15:17], lg[:, 15:17], AF.Exp, bias=lg[:, MX2:MX2 + 1]))
                    nc.vector.wait_ge(s_sce.h, e2_)
                    v_ = s_dve.inc(nc.vector.reduce_sum(lg[:, S15:S15 + 1], lg[:, 0:15], axis=AX))
                    nc.vector.wait_ge(s_dve.h, v_)
                    v_ = s_dve.inc(nc.vector.reciprocal(lg[:, RC15:RC15 + 1], lg[:, S15:S15 + 1]))
                    nc.vector.wait_ge(s_dve.h, v_)
                    v_ = s_dve.inc(nc.vector.max(out=m8[:, :], in_=lg[:, 0:15]))
                    nc.vector.wait_ge(s_dve.h, v_)
                    v_ = s_dve.inc(nc.vector.tensor_tensor(lg[:, T3:T3 + 1], m8[:, 0:1], m8[:, 1:2], OP.add))
                    nc.vector.wait_ge(s_dve.h, v_)
                    v_ = s_dve.inc(nc.vector.tensor_tensor(lg[:, T3:T3 + 1], lg[:, T3:T3 + 1], m8[:, 2:3], OP.add))
                    nc.vector.wait_ge(s_dve.h, v_)
                    v_ = s_dve.inc(nc.vector.tensor_tensor(lg[:, T3:T3 + 1], lg[:, T3:T3 + 1], lg[:, RC15:RC15 + 1], OP.mult))
                    v_ = s_dve.inc(nc.vector.tensor_tensor(lg[:, SW0:SW0 + 1], lg[:, 15:16], lg[:, 16:17], OP.add))
                    nc.vector.wait_ge(s_dve.h, v_)
                    v_ = s_dve.inc(nc.vector.reciprocal(lg[:, SW1:SW1 + 1], lg[:, SW0:SW0 + 1]))
                    nc.vector.wait_ge(s_dve.h, v_)
                    v_ = s_dve.inc(nc.vector.tensor_tensor(lg[:, SW0:SW0 + 1], lg[:, 15:16], lg[:, SW1:SW1 + 1], OP.mult))
                    nc.vector.wait_ge(s_dve.h, v_)
                    v_ = s_dve.inc(nc.vector.tensor_tensor(lg[:, SW1:SW1 + 1], lg[:, 16:17], lg[:, SW1:SW1 + 1], OP.mult))
                    nc.vector.wait_ge(s_dve.h, v_)
                    v_ = s_dve.inc(nc.vector.tensor_tensor(lg[:, GTE:GTE + 1], lg[:, SW1:SW1 + 1], lg[:, T3:T3 + 1], OP.mult))
                    nc.vector.wait_ge(s_dve.h, v_)
                    v_ = s_dve.inc(nc.vector.tensor_scalar(lg[:, GTE:GTE + 1], lg[:, GTE:GTE + 1], 3.0, None, OP.mult))
                    nc.vector.wait_ge(s_dve.h, v_)
                    v_ = s_dve.inc(nc.vector.tensor_tensor(lg[:, GTE:GTE + 1], lg[:, GTE:GTE + 1], lg[:, SW0:SW0 + 1], OP.add))
                    nc.vector.wait_ge(s_dve.h, v_)
                    v_ = s_dve.inc(nc.vector.tensor_scalar(gcolb[:, 0:1], lg[:, GTE:GTE + 1], 2.0, None, OP.mult))
                    nc.tensor.wait_ge(s_dve.h, v_)
                    nc.tensor.wait_ge(s_dve.h, grfree)
                    tv = s_te.inc(nc.tensor.transpose(pT[0:1, 0:128], gcolb[:, 0:1], ident[:, :]))
                    nc.vector.wait_ge(s_te.h, tv)
                    gv = s_dve.inc(nc.vector.tensor_copy(grow[0:1, :], pT[0:1, 0:128]))
                    nc.tensor.wait_ge(s_dve.h, gv)
                    bcv = s_te.inc(nc.tensor.matmul(pb[1][:, 0:128], onesb[0:1, 0:128], grow[0:1, :],
                                                    start=True, stop=True))
                    nc.vector.wait_ge(s_te.h, bcv)
                    hg3 = hg8[:, :].rearrange("p (a q) -> p a q", a=8)
                    hT3 = r3(ht[:, :], 8)
                    gb3 = pb[1][:, 0:128].unsqueeze(1).broadcast_to([128, 8, 128])
                    grfree = s_dve.inc(nc.vector.tensor_tensor(hg3, hT3, gb3, OP.mult))
                    hTtfree[tc % 2] = grfree
                    for hh in range(2):
                        nc.sync.wait_ge(s_dve.h, yacfree[hh])
                        ya_v = s_ld.inc(nc.sync.dma_start(yac[hh][:, :],
                                                          ya_dram[tc * 128:(tc + 1) * 128, hh * 512:(hh + 1) * 512]), 16)
                        nc.tensor.wait_ge(s_dve.h, grfree)
                        nc.tensor.wait_ge(s_dve.h, yfree2)
                        nc.tensor.wait_ge(s_ldw.h, wv1)
                        for i in range(8):
                            nc.tensor.matmul(pb[2][:, :], hg8[:, i * 128:(i + 1) * 128],
                                             fpw[:, i * D + hh * 512: i * D + hh * 512 + 512],
                                             start=(i == 0), stop=False)
                        yi = nc.tensor.matmul(pb[2][:, :], onesb[0:1, 0:128], bprow[0:1, hh * 512:hh * 512 + 512],
                                              start=False, stop=True)
                        ytv = s_te.inc(yi)
                        nc.vector.wait_ge(s_te.h, ytv)
                        nc.vector.wait_ge(s_ld.h, ya_v)
                        nc.vector.wait_ge(s_st.h, outfree[hh])
                        yev = s_dve.inc(nc.vector.tensor_tensor(outb[hh][:, :], yac[hh][:, :], pb[2][:, :], OP.add))
                        yfree2 = yev
                        yacfree[hh] = yev
                        nc.sync.wait_ge(s_dve.h, yev)
                        outfree[hh] = s_st.inc(nc.sync.dma_start(
                            out[tc * 128:(tc + 1) * 128, hh * 512:(hh + 1) * 512], outb[hh][:, :]), 16)

        if "stats" in tap_aps:
            s_st.inc(nc.sync.dma_start(tap_aps["stats"][:, 0:16], stats_s1[:, :]), 16)
            s_st.inc(nc.sync.dma_start(tap_aps["stats"][:, 16:32], stats_s2[:, :]), 16)
        if "xnT" in tap_aps:
            s_st.inc(nc.sync.dma_start(tap_aps["xnT"][:, :], xnT_dram[:, :]), 16)
        nc.sync.wait_ge(s_st.h, s_st.n)

    return nc, tap_aps



def _prep_shared(inputs):
    f = np.float32
    g = {}
    tr = lambda a: np.ascontiguousarray(np.asarray(a, f).T)
    g["wq_t"] = tr(inputs["L0_attn_WQ"]); g["wk_t"] = tr(inputs["L0_attn_WK"])
    g["wv_t"] = tr(inputs["L0_attn_WV"]); g["wg_t"] = tr(inputs["L0_attn_Wg"])
    g["wo_t"] = tr(inputs["L0_attn_WO"])
    g["fq_t"] = tr(inputs["L0_ffn_Wq"]); g["fk_t"] = tr(inputs["L0_ffn_Wk"])
    g["fv_t"] = tr(inputs["L0_ffn_Wv"]); g["fp_t"] = tr(inputs["L0_ffn_Wp"])
    g["wr_t"] = tr(inputs["L0_ffn_Wr"]); g["ws_t"] = tr(inputs["L0_ffn_Ws"])
    col = lambda b: np.ascontiguousarray(np.asarray(b, f).reshape(8, 128).T)
    g["bq_c"] = col(inputs["L0_attn_bQ"]); g["bk_c"] = col(inputs["L0_attn_bK"])
    g["bg_c"] = col(inputs["L0_attn_bg"])
    g["fbq_c"] = col(inputs["L0_ffn_bq"]); g["fbk_c"] = col(inputs["L0_ffn_bk"])
    g["bv_att_row"] = np.asarray(inputs["L0_attn_bV"], f).reshape(1, D).copy()
    g["fbv_row"] = np.asarray(inputs["L0_ffn_bv"], f).reshape(1, D).copy()
    g["bo_row"] = np.asarray(inputs["L0_attn_bO"], f).reshape(1, D).copy()
    g["bp_row"] = np.asarray(inputs["L0_ffn_bp"], f).reshape(1, D).copy()
    g["brs_row"] = np.concatenate([np.asarray(inputs["L0_ffn_br"], f).ravel(),
                                   np.asarray(inputs["L0_ffn_bs"], f).ravel()]).reshape(1, 17).copy()
    g["gamma_c"] = col(inputs["L0_attn_ln_gamma"]); g["beta_c"] = col(inputs["L0_attn_ln_beta"])
    pairstack = lambda a: np.ascontiguousarray(
        np.transpose(np.asarray(a, f).reshape(8, 2, 64), (1, 2, 0)).reshape(128, 8))
    g["al_t2"] = pairstack(inputs["L0_attn_alpha_logit"])
    g["qe_t2"] = pairstack(np.asarray(inputs["L0_ffn_query_embedding"], f)[:, 0, :])
    g["temp_row"] = np.asarray(inputs["L0_ffn_temperature"], f).reshape(1, 16).copy()
    return g


def make_in_maps(inputs):
    shared = _prep_shared(inputs)
    x = np.asarray(inputs["L0_attn_x"], np.float32)
    in_maps = []
    for b in range(B):
        for j in range(2):
            m = dict(shared)
            m["x_own"] = np.ascontiguousarray(x[b, j * HALF:(j + 1) * HALF])
            m["x_other"] = np.ascontiguousarray(x[b, (1 - j) * HALF:(2 - j) * HALF])
            m["chain_scale"] = np.full((128, 1), float(j), np.float32)
            in_maps.append(m)
    return in_maps


def kernel(**inputs):
    nc, _ = build_nc()
    in_maps = make_in_maps(inputs)
    res = run_bass_kernel_spmd(nc, in_maps, core_ids=list(range(8)))
    o = np.empty((B, T, D), np.float32)
    for b in range(B):
        for j in range(2):
            o[b, j * HALF:(j + 1) * HALF] = res.results[b * 2 + j]["out"]
    return o


# version 11
# speedup vs baseline: 1.0533x; 1.0533x over previous
import numpy as np
import ml_dtypes
from contextlib import ExitStack

import concourse.bass as bass
import concourse.mybir as mybir
from concourse.bass_utils import run_bass_kernel_spmd

F32 = mybir.dt.float32
BF16 = mybir.dt.bfloat16
AF = mybir.ActivationFunctionType
OP = mybir.AluOpType
AX = mybir.AxisListType.X

B, T, D = 4, 4096, 1024
HALF = T // 2
NH, DH = 16, 64
CH = 128
EPS = 1e-5
INV_D = 1.0 / D


class Sem:
    def __init__(self, es, nc, name):
        self.h = es.enter_context(nc.semaphore(name))
        self.n = 0

    def inc(self, inst, k=1):
        inst.then_inc(self.h, k)
        self.n += k
        return self.n


def r3(ap2d, c):
    return ap2d.rearrange("p (c q) -> p c q", c=c)




def _finish(nc, s_st, s_dve, s_sce, s_te, tap_aps, s_ld, xn2T=None, stats=None, xnT_dram=None):
    for e in [nc.sync, nc.scalar, nc.vector, nc.tensor, nc.gpsimd]:
        for s in [s_ld, s_te, s_dve, s_sce, s_st]:
            e.wait_ge(s.h, s.n)
    if xn2T is not None and "xn2T" in tap_aps:
        s_st.inc(nc.sync.dma_start(tap_aps["xn2T"][:, :], xn2T[:, :]), 16)
    if stats is not None and "stats" in tap_aps:
        s_st.inc(nc.sync.dma_start(tap_aps["stats"][:, 0:16], stats[0][:, :]), 16)
        s_st.inc(nc.sync.dma_start(tap_aps["stats"][:, 16:32], stats[1][:, :]), 16)
    if xnT_dram is not None and "xnT" in tap_aps:
        s_st.inc(nc.sync.dma_start(tap_aps["xnT"][:, :], xnT_dram[:, :]), 16)
    nc.sync.wait_ge(s_st.h, s_st.n)

def build_nc(taps=(), upto="full", p2_lim=16, p2_scan=2, hs=4):
    LV = {"p1": 1, "p0b": 2, "p2": 3, "p3a": 4, "p3b": 5, "full": 6}[upto]
    nc = bass.Bass("TRN2", target_bir_lowering=False)
    dt = nc.dram_tensor

    x_own = dt("x_own", [HALF, D], F32, kind="ExternalInput").ap()
    x_other = dt("x_other", [HALF, D], F32, kind="ExternalInput").ap()
    chain_scale = dt("chain_scale", [128, 1], F32, kind="ExternalInput").ap()
    wts = {}
    for nm in ["wq", "wk", "wv", "wg", "wo", "fq", "fk", "fv", "fp"]:
        wts[nm] = dt(nm + "_t", [D, D], F32, kind="ExternalInput").ap()
    wr_t = dt("wr_t", [D, 15], F32, kind="ExternalInput").ap()
    ws_t = dt("ws_t", [D, 2], F32, kind="ExternalInput").ap()
    bcols = {}
    for nm in ["bq", "bk", "bg", "fbq", "fbk"]:
        bcols[nm] = dt(nm + "_c", [128, 8], F32, kind="ExternalInput").ap()
    bv_att_row = dt("bv_att_row", [1, D], F32, kind="ExternalInput").ap()
    fbv_row = dt("fbv_row", [1, D], F32, kind="ExternalInput").ap()
    bo_row = dt("bo_row", [1, D], F32, kind="ExternalInput").ap()
    bp_row = dt("bp_row", [1, D], F32, kind="ExternalInput").ap()
    brs_row = dt("brs_row", [1, 17], F32, kind="ExternalInput").ap()
    gamma_c = dt("gamma_c", [128, 8], F32, kind="ExternalInput").ap()
    beta_c = dt("beta_c", [128, 8], F32, kind="ExternalInput").ap()
    al_t2 = dt("al_t2", [128, 8], F32, kind="ExternalInput").ap()
    qe_t2 = dt("qe_t2", [128, 8], F32, kind="ExternalInput").ap()
    temp_row = dt("temp_row", [1, 16], F32, kind="ExternalInput").ap()

    out = dt("out", [HALF, D], F32, kind="ExternalOutput").ap()
    tap_aps = {}
    for name, shape, dtp in [("xn2T", [128, 8 * T], BF16), ("xnT", [128, 8 * T], BF16),
                             ("stats", [128, 32], F32), ("ya", [HALF, D], F32),
                             ("patq1", [128, 512], F32), ("hT", [128, 8 * HALF], BF16),
                             ("fkT", [128, 8 * T], BF16), ("fvT", [128, 32 * 1024], BF16),
                             ("fqT", [128, 8 * HALF], BF16), ("negM", [128, 16], F32),
                             ("blkK", [128, 2048], BF16), ("blkV", [128, 2048], BF16),
                             ("blkQ", [128, 2048], BF16), ("blkG", [128, 2048], BF16),
                             ("og8", [128, 1024], BF16), ("Hf", [128, 512], F32),
                             ("qt8", [128, 1024], BF16), ("kt8", [128, 1024], BF16),
                             ("kbtok", [128, 1024], BF16), ("Am", [128, 512], BF16)]:
        if name in taps:
            tap_aps[name] = dt("tap_" + name, shape, dtp, kind="ExternalOutput").ap()

    xnT_dram = dt("xnT_s", [128, 8 * T], BF16, kind="Internal").ap()
    ya_dram = dt("ya_s", [HALF, D], F32, kind="Internal").ap()
    hT_dram = dt("hT_s", [128, 8 * HALF], BF16, kind="Internal").ap()

    ident_np = np.eye(128, dtype=ml_dtypes.bfloat16)
    ones_np = np.ones((128, 128), dtype=ml_dtypes.bfloat16)
    onesf_np = np.ones((128, 128), dtype=np.float32)
    mask_np = np.triu(np.ones((128, 128), np.float32))
    r64 = np.arange(64, dtype=np.float32)
    s128 = np.arange(128, dtype=np.float32)
    E_q1 = np.broadcast_to(r64 + 1.0, (128, 64)).copy()
    E_k = np.broadcast_to(-(r64 + 1.0), (128, 64)).copy()
    E_kb = np.broadcast_to(127.0 - s128, (128, 128)).copy()

    es = ExitStack()
    with es:
        sb = lambda name, shape, dtp: es.enter_context(nc.sbuf_tensor(name, shape, dtp))
        es12 = ExitStack()
        sb12 = lambda name, shape, dtp: es12.enter_context(nc.sbuf_tensor(name, shape, dtp))
        ident = sb("ident", [128, 128], BF16)
        onesb = sb("onesb", [128, 128], BF16)
        onesf = sb("onesf", [128, 128], F32)
        maskb = sb("maskb", [128, 128], F32)
        e_q1 = sb("e_q1", [128, 64], F32)
        e_k = sb("e_k", [128, 64], F32)
        e_kb = sb("e_kb", [128, 128], F32)
        lna = sb("lna", [128, 8], F32)
        alC = sb("alC", [128, 8], F32)
        al64 = sb("al64", [128, 8], F32)
        invab = sb("invab", [128, 8], BF16)
        al65b = sb("al65b", [128, 8], BF16)
        qecol = sb("qecol", [128, 8], F32)
        csc = sb("csc", [128, 1], F32)
        gcol = sb("gcol", [128, 8], F32)
        bcol_f32 = {nm: sb(nm + "_f", [128, 8], F32) for nm in bcols}
        beta_bf = sb("beta_bf", [128, 8], BF16)
        fbvrow = sb("fbvrow", [1, D], BF16)
        borow = sb("borow", [1, D], BF16)
        bobp = sb("bobp", [1, D], BF16)
        bprow = sb("bprow", [1, D], BF16)
        brsrow = sb("brsrow", [1, 17], BF16)
        sp8 = sb("sp8", [1, 16], F32)
        spb = sb("spb", [128, 16], F32)
        spp = sb("spp", [128, 8], F32)
        negM = sb("negM", [128, 16], F32)
        negMp = sb("negMp", [128, 8], F32)
        stats_s1 = sb("stats_s1", [128, 16], F32)
        stats_s2 = sb("stats_s2", [128, 16], F32)
        bvarow = sb12("bvarow", [1, D], F32)
        bvarow_b = sb12("bvarow_b", [1, D], BF16)
        xn2T = sb12("xn2T", [128, 8 * T], BF16)

        S2 = [es.enter_context(nc.psum_tensor(f"s2_{i}", [128, 1024], F32)) for i in range(2)]
        pb_single = [es.enter_context(nc.psum_tensor(f"pb{i}", [128, 512], F32)) for i in range(3)]
        pT = es.enter_context(nc.psum_tensor("pT", [128, 1024], BF16))
        pb = [S2[0].ap()[:, 0:512], S2[0].ap()[:, 512:1024],
              S2[1].ap()[:, 0:512], S2[1].ap()[:, 512:1024],
              pb_single[0].ap()[:, :], pb_single[1].ap()[:, :], pb_single[2].ap()[:, :]]

        s_ld = Sem(es, nc, "s_ld")
        s_ldw = Sem(es, nc, "s_ldw")
        s_te = Sem(es, nc, "s_te")
        s_dve = Sem(es, nc, "s_dve")
        s_sce = Sem(es, nc, "s_sce")
        s_st = Sem(es, nc, "s_st")
        ALL_SEMS = [s_ld, s_ldw, s_te, s_dve, s_sce, s_st]

        def barrier():
            for e in [nc.sync, nc.scalar, nc.vector, nc.tensor, nc.gpsimd]:
                for s in ALL_SEMS:
                    e.wait_ge(s.h, s.n)

        ic = nc.inline_tensor
        for dst, arr, cn in [(ident, ident_np, "c_id"), (onesb, ones_np, "c_on"),
                             (onesf, onesf_np, "c_onf"), (maskb, mask_np, "c_mk"),
                             (e_q1, E_q1, "c_eq1"), (e_k, E_k, "c_ek"), (e_kb, E_kb, "c_ekb")]:
            s_ld.inc(nc.sync.dma_start(dst[:, :], ic(arr, cn).ap()[:, :]), 16)
        for nm in bcols:
            s_ld.inc(nc.sync.dma_start(bcol_f32[nm][:, :], bcols[nm][:, :]), 16)
        s_ld.inc(nc.sync.dma_start(gcol[:, :], gamma_c[:, :]), 16)
        s_ld.inc(nc.sync.dma_start(lna[:, :], al_t2[:, :]), 16)
        s_ld.inc(nc.sync.dma_start(csc[:, :], chain_scale[:, :]), 16)
        s_ld.inc(nc.sync.dma_start(sp8[:, :], temp_row[:, :]), 16)
        s_ld.inc(nc.sync.dma_start(bvarow[:, :], bv_att_row[:, :]), 16)
        s_ldw.inc(nc.gpsimd.dma_start(beta_bf[:, :], beta_c[:, :]), 16)
        s_ld.inc(nc.sync.dma_start(qecol[:, :], qe_t2[:, :]), 16)
        s_ldw.inc(nc.gpsimd.dma_start(borow[:, :], bo_row[:, :]), 16)
        s_ldw.inc(nc.gpsimd.dma_start(bprow[:, :], bp_row[:, :]), 16)
        s_ldw.inc(nc.gpsimd.dma_start(brsrow[:, :], brs_row[:, :]), 16)
        s_ldw.inc(nc.gpsimd.dma_start(fbvrow[:, :], fbv_row[:, :]), 16)
        small_lds = s_ld.n
        small_ldw = s_ldw.n

        w_attn = {nm: sb12(nm + "_w", [128, 8 * D], BF16) for nm in ["wq", "wk", "wv", "wg", "wo"]}
        for nm in ["wq", "wk", "wv", "wg", "wo"]:
            s_ldw.inc(nc.gpsimd.dma_start(r3(w_attn[nm][:, :], 8),
                                          wts[nm][:, :].rearrange("(i p) o -> p i o", p=128)), 16)
        attn_w_ld = s_ldw.n

        nc.scalar.wait_ge(s_ld.h, small_lds)
        nc.scalar.wait_ge(s_ldw.h, small_ldw)
        v = s_sce.inc(nc.scalar.activation(alC[:, :], lna[:, :], AF.Exp, scale=-1.0))
        nc.vector.wait_ge(s_sce.h, v)
        v = s_dve.inc(nc.vector.tensor_scalar_add(alC[:, :], alC[:, :], 1.0))
        nc.scalar.wait_ge(s_dve.h, v)
        v = s_sce.inc(nc.scalar.activation(lna[:, :], alC[:, :], AF.Ln))
        nc.scalar.wait_ge(s_sce.h, v)
        v = s_sce.inc(nc.scalar.mul(lna[:, :], lna[:, :], -1.0))
        nc.scalar.wait_ge(s_sce.h, v)
        nc.scalar.activation(alC[:, :], lna[:, :], AF.Exp, scale=float(CH))
        nc.scalar.activation(al64[:, :], lna[:, :], AF.Exp, scale=64.0)
        nc.scalar.activation(invab[:, :], lna[:, :], AF.Exp, scale=-1.0)
        nc.scalar.activation(al65b[:, :], lna[:, :], AF.Exp, scale=65.0)
        v = s_sce.inc(nc.scalar.activation(sp8[:, :], sp8[:, :], AF.Exp))
        nc.vector.wait_ge(s_sce.h, v)
        v = s_dve.inc(nc.vector.tensor_scalar_add(sp8[:, :], sp8[:, :], 1.0))
        nc.scalar.wait_ge(s_dve.h, v)
        v = s_sce.inc(nc.scalar.activation(sp8[:, :], sp8[:, :], AF.Ln))
        nc.scalar.wait_ge(s_sce.h, v)
        sp8_done = s_sce.inc(nc.scalar.mul(sp8[:, :], sp8[:, :], 0.125))
        nc.tensor.wait_ge(s_sce.h, sp8_done)
        nc.tensor.wait_ge(s_ld.h, small_lds)
        v = s_te.inc(nc.tensor.matmul(pb[0][:, 0:16], onesf[0:1, 0:128], sp8[0:1, :],
                                      start=True, stop=True))
        nc.vector.wait_ge(s_te.h, v)
        nc.vector.wait_ge(s_ldw.h, small_ldw)
        s_dve.inc(nc.vector.tensor_tensor(bobp[0:1, :], borow[0:1, :], bprow[0:1, :], OP.add))
        v = s_dve.inc(nc.vector.tensor_copy(spb[:, :], pb[0][:, 0:16]))
        nc.vector.wait_ge(s_dve.h, v)
        nc.vector.tensor_copy(spp[0:64, :], spb[0:64, 0:16].rearrange("p (a b) -> p a b", b=2)[:, :, 0:1].squeeze(2))
        s_dve.inc(nc.vector.tensor_copy(spp[64:128, :], spb[64:128, 0:16].rearrange("p (a b) -> p a b", b=2)[:, :, 1:2].squeeze(2)))
        decay_sce = s_sce.n

        with ExitStack() as es1:
            sb1 = lambda name, shape, dtp: es1.enter_context(nc.sbuf_tensor(name, shape, dtp))
            xt2 = [sb1(f"xt2_{i}", [128, 2048], F32) for i in range(2)]
            xnf = sb1("xnf", [128, 2048], F32)
            xnb = sb1("xnb", [128, 2048], BF16)
            xn2b = sb1("xn2b", [128, 2048], BF16)
            stage = [sb1(f"xnTs{i}", [128, 2048], BF16) for i in range(2)]
            cols = sb1("cols", [128, 24], F32)
            SX, SX2, MU, VAR, R1, SXN, SXN2, MU2, VAR2, R2, TMP = 0, 2, 4, 6, 8, 10, 12, 14, 16, 18, 20
            xfree_sce = {0: 0, 1: 0}
            xfree_dve = {0: 0, 1: 0}
            stfree = {0: 0, 1: 0}
            st_count = 0

            for g in range(16):
                bi = g % 2
                t0 = 2 * g
                src_ = x_other if t0 < 16 else x_own
                o = (t0 if t0 < 16 else t0 - 16) * 128
                if g >= 2:
                    nc.sync.wait_ge(s_sce.h, xfree_sce[bi])
                    nc.sync.wait_ge(s_dve.h, xfree_dve[bi])
                ld_v = s_ld.inc(nc.sync.dma_start(
                    xt2[bi][:, :].rearrange("p (c d) -> p c d", c=2),
                    src_[o:o + 256, :].rearrange("(c p) d -> p c d", p=128)), 16)
                nc.scalar.wait_ge(s_ld.h, ld_v)
                for ti in range(2):
                    sq1 = s_sce.inc(nc.scalar.activation(xnf[:, ti * 1024:(ti + 1) * 1024],
                                                         xt2[bi][:, ti * 1024:(ti + 1) * 1024], AF.Square,
                                                         accum_out=cols[:, SX2 + ti:SX2 + ti + 1]))
                nc.vector.wait_ge(s_ld.h, ld_v)
                for ti in range(2):
                    rv_ = s_dve.inc(nc.vector.reduce_sum(cols[:, SX + ti:SX + ti + 1],
                                                         xt2[bi][:, ti * 1024:(ti + 1) * 1024], axis=AX))
                nc.vector.wait_ge(s_dve.h, rv_)
                nc.vector.wait_ge(s_sce.h, sq1)
                v_ = s_dve.inc(nc.vector.tensor_scalar_mul(cols[:, MU:MU + 2], cols[:, SX:SX + 2], INV_D))
                nc.vector.tensor_scalar_mul(cols[:, VAR:VAR + 2], cols[:, SX2:SX2 + 2], INV_D)
                nc.vector.wait_ge(s_dve.h, v_)
                v_ = s_dve.inc(nc.vector.tensor_tensor(cols[:, TMP:TMP + 2], cols[:, MU:MU + 2], cols[:, MU:MU + 2], OP.mult))
                nc.vector.wait_ge(s_dve.h, v_)
                v_ = s_dve.inc(nc.vector.tensor_tensor(cols[:, VAR:VAR + 2], cols[:, VAR:VAR + 2], cols[:, TMP:TMP + 2], OP.subtract))
                nc.vector.wait_ge(s_dve.h, v_)
                st1 = s_dve.inc(nc.vector.tensor_scalar_add(cols[:, VAR:VAR + 2], cols[:, VAR:VAR + 2], EPS))
                nc.scalar.wait_ge(s_dve.h, st1)
                v_ = s_sce.inc(nc.scalar.activation(cols[:, R1:R1 + 2], cols[:, VAR:VAR + 2], AF.Ln))
                nc.scalar.wait_ge(s_sce.h, v_)
                r1v = s_sce.inc(nc.scalar.activation(cols[:, R1:R1 + 2], cols[:, R1:R1 + 2], AF.Exp, scale=-0.5))
                nc.vector.wait_ge(s_sce.h, r1v)
                for ti in range(2):
                    xnfv = s_dve.inc(nc.vector.tensor_scalar(
                        xnf[:, ti * 1024:(ti + 1) * 1024], xt2[bi][:, ti * 1024:(ti + 1) * 1024],
                        cols[:, MU + ti:MU + ti + 1], cols[:, R1 + ti:R1 + ti + 1], OP.subtract, OP.mult))
                xfree_dve[bi] = xnfv
                xfree_sce[bi] = sq1
                nc.scalar.wait_ge(s_dve.h, xnfv)
                for ti in range(2):
                    sq2 = s_sce.inc(nc.scalar.activation(xnb[:, ti * 1024:(ti + 1) * 1024],
                                                         xnf[:, ti * 1024:(ti + 1) * 1024], AF.Square,
                                                         accum_out=cols[:, SXN2 + ti:SXN2 + ti + 1]))
                nc.vector.wait_ge(s_sce.h, sq2)
                for ti in range(2):
                    v_ = s_dve.inc(nc.vector.reduce_sum(cols[:, SXN + ti:SXN + ti + 1],
                                                        xnf[:, ti * 1024:(ti + 1) * 1024], axis=AX))
                nc.vector.wait_ge(s_dve.h, v_)
                v_ = s_dve.inc(nc.vector.tensor_scalar_mul(cols[:, MU2:MU2 + 2], cols[:, SXN:SXN + 2], INV_D))
                nc.vector.tensor_scalar_mul(cols[:, VAR2:VAR2 + 2], cols[:, SXN2:SXN2 + 2], INV_D)
                nc.vector.wait_ge(s_dve.h, v_)
                v_ = s_dve.inc(nc.vector.tensor_tensor(cols[:, TMP:TMP + 2], cols[:, MU2:MU2 + 2], cols[:, MU2:MU2 + 2], OP.mult))
                nc.vector.wait_ge(s_dve.h, v_)
                v_ = s_dve.inc(nc.vector.tensor_tensor(cols[:, VAR2:VAR2 + 2], cols[:, VAR2:VAR2 + 2], cols[:, TMP:TMP + 2], OP.subtract))
                nc.vector.wait_ge(s_dve.h, v_)
                st3 = s_dve.inc(nc.vector.tensor_scalar_add(cols[:, VAR2:VAR2 + 2], cols[:, VAR2:VAR2 + 2], EPS))
                nc.scalar.wait_ge(s_dve.h, st3)
                v_ = s_sce.inc(nc.scalar.activation(cols[:, R2:R2 + 2], cols[:, VAR2:VAR2 + 2], AF.Ln))
                nc.scalar.wait_ge(s_sce.h, v_)
                r2v = s_sce.inc(nc.scalar.activation(cols[:, R2:R2 + 2], cols[:, R2:R2 + 2], AF.Exp, scale=-0.5))
                nc.vector.wait_ge(s_sce.h, r2v)
                nc.vector.tensor_copy(xnb[:, :], xnf[:, :])
                for ti in range(2):
                    nc.vector.tensor_scalar(xn2b[:, ti * 1024:(ti + 1) * 1024], xnf[:, ti * 1024:(ti + 1) * 1024],
                                            cols[:, MU2 + ti:MU2 + ti + 1], cols[:, R2 + ti:R2 + ti + 1],
                                            OP.subtract, OP.mult)
                if t0 >= 16:
                    oc = t0 - 16
                    nc.vector.tensor_scalar_add(stats_s1[:, oc:oc + 2], cols[:, R1:R1 + 2], 1.0)
                    nc.vector.tensor_tensor(stats_s2[:, oc:oc + 2], cols[:, MU:MU + 2], cols[:, R1:R1 + 2], OP.mult)
                bufv = s_dve.inc(nc.vector.engine_nop())
                nc.tensor.wait_ge(s_dve.h, bufv)
                nc.vector.wait_ge(s_st.h, stfree[bi])
                for rnd in range(2):
                    for c in range(8):
                        i1 = nc.tensor.transpose(pT[:, c * 128:(c + 1) * 128],
                                                 xnb[:, rnd * 1024 + c * 128: rnd * 1024 + (c + 1) * 128], ident[:, :])
                    te1 = s_te.inc(i1)
                    nc.vector.wait_ge(s_te.h, te1)
                    for c in range(8):
                        i2 = nc.vector.tensor_copy(stage[bi][:, rnd * 1024 + c * 128: rnd * 1024 + (c + 1) * 128],
                                                   pT[:, c * 128:(c + 1) * 128])
                    ev1 = s_dve.inc(i2)
                    nc.tensor.wait_ge(s_dve.h, ev1)
                nc.sync.wait_ge(s_dve.h, ev1)
                for ti in range(2):
                    st_count = s_st.inc(nc.sync.dma_start(
                        r3(xnT_dram[:, :], 8)[:, :, (t0 + ti) * 128:(t0 + ti + 1) * 128],
                        r3(stage[bi][:, ti * 1024:(ti + 1) * 1024], 8)), 16)
                stfree[bi] = st_count
                for rnd in range(2):
                    for c in range(8):
                        i3 = nc.tensor.transpose(pT[:, c * 128:(c + 1) * 128],
                                                 xn2b[:, rnd * 1024 + c * 128: rnd * 1024 + (c + 1) * 128], ident[:, :])
                    te2 = s_te.inc(i3)
                    nc.vector.wait_ge(s_te.h, te2)
                    tt = t0 + rnd
                    for c in range(8):
                        i4 = nc.vector.tensor_copy(xn2T[:, c * T + tt * 128: c * T + (tt + 1) * 128],
                                                   pT[:, c * 128:(c + 1) * 128])
                    ev2 = s_dve.inc(i4)
                    nc.tensor.wait_ge(s_dve.h, ev2)
            nc.sync.wait_ge(s_st.h, st_count)

        if LV < 2:
            for name in list(tap_aps):
                pass
            _finish(nc, s_st, s_dve, s_sce, s_te, tap_aps, s_ld, xn2T=xn2T, stats=(stats_s1, stats_s2), xnT_dram=xnT_dram)
            es12.close()
            return nc, tap_aps
        nc.tensor.wait_ge(s_ldw.h, attn_w_ld)
        biasf = {nm: sb12("bias_" + nm, [128, 8], F32) for nm in ["bq", "bk", "bg"]}
        bmfree = [0, 0]
        for wnm, bnm in [("wq", "bq"), ("wk", "bk"), ("wg", "bg")]:
            w = w_attn[wnm]
            for p in range(8):
                bank = pb[p % 2]
                nc.tensor.wait_ge(s_dve.h, bmfree[p % 2])
                for i in range(8):
                    mi = nc.tensor.matmul(bank[:, p:p + 1], w[:, i * D + p * 128: i * D + (p + 1) * 128],
                                          beta_bf[:, i:i + 1], start=(i == 0), stop=(i == 7))
                bv_ = s_te.inc(mi)
                nc.vector.wait_ge(s_te.h, bv_)
                bmfree[p % 2] = s_dve.inc(nc.vector.tensor_add(biasf[bnm][:, p:p + 1], bank[:, p:p + 1],
                                                               bcol_f32[bnm][:, p:p + 1]))
        nc.tensor.wait_ge(s_dve.h, max(bmfree))
        for hh in range(2):
            bank = pb[hh]
            for i in range(8):
                mi = nc.tensor.matmul(bank[0:1, 0:512], beta_bf[:, i:i + 1],
                                      w_attn["wv"][:, i * D + hh * 512: i * D + hh * 512 + 512],
                                      start=(i == 0), stop=(i == 7))
            bv_ = s_te.inc(mi)
            nc.vector.wait_ge(s_te.h, bv_)
            fv_ = s_dve.inc(nc.vector.tensor_tensor(bvarow_b[0:1, hh * 512:(hh + 1) * 512],
                                                    bank[0:1, 0:512], bvarow[0:1, hh * 512:(hh + 1) * 512], OP.add))
        nc.tensor.wait_ge(s_dve.h, fv_)
        for wnm in ["wq", "wk", "wv", "wg"]:
            for i in range(8):
                gi = nc.vector.tensor_scalar_mul(w_attn[wnm][:, i * D:(i + 1) * D],
                                                 w_attn[wnm][:, i * D:(i + 1) * D], gcol[:, i:i + 1])
        fold_done = s_dve.inc(gi)
        pat_q1 = sb12("pat_q1", [128, 512], BF16)
        pat_k = sb12("pat_k", [128, 512], BF16)
        pat_kb = sb12("pat_kb", [128, 1024], BF16)
        with ExitStack() as esp:
            patf = esp.enter_context(nc.sbuf_tensor("patf", [128, 2048], F32))
            nc.vector.wait_ge(s_sce.h, decay_sce)
            for p in range(8):
                nc.vector.tensor_scalar_mul(patf[:, p * 64:(p + 1) * 64], e_q1[:, :], lna[:, p:p + 1])
                nc.vector.tensor_scalar_mul(patf[:, 512 + p * 64: 512 + (p + 1) * 64], e_k[:, :], lna[:, p:p + 1])
                pi = nc.vector.tensor_scalar_mul(patf[:, 1024 + p * 128: 1024 + (p + 1) * 128], e_kb[:, :], lna[:, p:p + 1])
            pv = s_dve.inc(pi)
            nc.scalar.wait_ge(s_dve.h, pv)
            nc.scalar.activation(pat_q1[:, :], patf[:, 0:512], AF.Exp)
            nc.scalar.activation(pat_k[:, :], patf[:, 512:1024], AF.Exp)
            si = nc.scalar.activation(pat_kb[:, :], patf[:, 1024:2048], AF.Exp)
            pat_done = s_sce.inc(si)
            nc.vector.wait_ge(s_sce.h, pat_done)
        if "patq1" in tap_aps:
            nc.gpsimd.wait_ge(s_sce.h, pat_done)
            s_ldw.inc(nc.gpsimd.dma_start(tap_aps["patq1"][:, :], pat_q1[:, :]), 16)

        if LV < 3:
            _finish(nc, s_st, s_dve, s_sce, s_te, tap_aps, s_ld, xn2T=xn2T, stats=(stats_s1, stats_s2), xnT_dram=xnT_dram)
            es12.close()
            return nc, tap_aps
        barrier()
        with ExitStack() as es2:
            sb2 = lambda name, shape, dtp: es2.enter_context(nc.sbuf_tensor(name, shape, dtp))
            blkK = sb2("blkK", [128, 2048], BF16)
            blkV = sb2("blkV", [128, 2048], BF16)
            blkQ = sb2("blkQ", [128, 2048], BF16)
            blkG = sb2("blkG", [128, 2048], BF16)
            kbT = sb2("kbT", [128, 1024], BF16)
            kbtok = sb2("kbtok", [128, 1024], BF16)
            qt8 = sb2("qt8", [128, 1024], BF16)
            qh8 = sb2("qh8", [128, 512], BF16)
            kt8 = sb2("kt8", [128, 1024], BF16)
            kh8 = sb2("kh8", [128, 512], BF16)
            Am = sb2("Am", [128, 1024], BF16)
            og8 = sb2("og8", [128, 1024], BF16)
            Hf = sb2("Hf", [128, 512], F32)
            Hb = sb2("Hb", [128, 512], BF16)
            H64b = sb2("H64b", [128, 512], BF16)
            xtile = sb2("xtile", [128, D], F32)
            yst = [sb2(f"yst{i}", [128, 512], F32) for i in range(2)]

            alC3 = alC[:, :].unsqueeze(2).broadcast_to([128, 8, 64])
            al643 = al64[:, :].unsqueeze(2).broadcast_to([128, 8, 64])
            inva3 = invab[:, :].unsqueeze(2).broadcast_to([128, 8, 64])
            al653 = al65b[:, :].unsqueeze(2).broadcast_to([128, 8, 64])
            mask2 = maskb[:, :].unsqueeze(1).broadcast_to([128, 2, 128])
            pat_q1_4 = pat_q1[:, :].rearrange("p (a c) -> p a c", a=8).unsqueeze(2).broadcast_to([128, 8, 2, 64])
            pat_k_4 = pat_k[:, :].rearrange("p (a c) -> p a c", a=8).unsqueeze(2).broadcast_to([128, 8, 2, 64])

            mset = s_dve.inc(nc.vector.memset(Hf[:, :], 0.0))
            nc.vector.memset(pb[2][:, :], 0.0)
            nc.vector.memset(pb[3][:, :], 0.0)
            mset = s_dve.inc(nc.vector.memset(pb[4][:, :], 0.0))

            pbfree = [0, 0]
            pbfree_s = [0, 0]
            yfree = 0
            ogfree = [0, 0]
            dfree = 0
            ptfree = 0
            gguard = 0
            xfree = 0
            ystfree = [0, 0]
            sfree = [0, 0]

            def proj_feat(wb, biascol, dst, base_lo, silu=False):
                for p in range(8):
                    bank = pb[p % 2]
                    nc.tensor.wait_ge(s_dve.h, pbfree[p % 2])
                    nc.tensor.wait_ge(s_sce.h, pbfree_s[p % 2])
                    for i in range(8):
                        mi = nc.tensor.matmul(bank[:, 0:256], wb[:, i * D + p * 128: i * D + (p + 1) * 128],
                                              xn2T[:, i * T + base_lo: i * T + base_lo + 256],
                                              start=(i == 0), stop=(i == 7))
                    tv = s_te.inc(mi)
                    dst3 = r3(dst[:, :], 2)[:, :, p * 128:(p + 1) * 128]
                    src3 = r3(bank[:, 0:256], 2)
                    if silu:
                        nc.scalar.wait_ge(s_te.h, tv)
                        nc.scalar.wait_ge(s_dve.h, gguard)
                        ev = s_sce.inc(nc.scalar.activation(dst3, src3, AF.Silu, bias=biascol[:, p:p + 1]))
                        pbfree_s[p % 2] = ev
                        sfree[p % 2] = ev
                    else:
                        nc.vector.wait_ge(s_te.h, tv)
                        ev = s_dve.inc(nc.vector.tensor_scalar(dst3, src3, biascol[:, p:p + 1], None, OP.add))
                        pbfree[p % 2] = ev

            def proj_tok(wb, brow, dst, base_lo):
                for tc in range(2):
                    for fh in range(2):
                        bk_ = (tc * 2 + fh) % 2
                        bank = pb[bk_]
                        nc.tensor.wait_ge(s_dve.h, pbfree[bk_])
                        nc.tensor.wait_ge(s_sce.h, pbfree_s[bk_])
                        for i in range(8):
                            nc.tensor.matmul(bank[:, 0:512],
                                             xn2T[:, i * T + base_lo + tc * 128: i * T + base_lo + tc * 128 + 128],
                                             wb[:, i * D + fh * 512: i * D + fh * 512 + 512],
                                             start=(i == 0), stop=False)
                        mi = nc.tensor.matmul(bank[:, 0:512], onesb[0:1, 0:128],
                                              brow[0:1, fh * 512:fh * 512 + 512], start=False, stop=True)
                        tv = s_te.inc(mi)
                        nc.vector.wait_ge(s_te.h, tv)
                        ev = s_dve.inc(nc.vector.tensor_copy(dst[:, tc * 1024 + fh * 512: tc * 1024 + fh * 512 + 512],
                                                             bank[:, 0:512]))
                        pbfree[bk_] = ev

            for b in range(p2_lim):
                heavy = b >= 8
                base_lo = b * 256
                proj_feat(w_attn["wk"], biasf["bk"], blkK, base_lo)
                proj_tok(w_attn["wv"], bvarow_b, blkV, base_lo)
                if heavy:
                    proj_feat(w_attn["wq"], biasf["bq"], blkQ, base_lo)
                    proj_feat(w_attn["wg"], biasf["bg"], blkG, base_lo, silu=True)
                    nc.vector.wait_ge(s_sce.h, max(sfree))
                for c in range(p2_scan):
                    cab = 2 * b + c
                    csl = slice(c * 1024, (c + 1) * 1024)
                    ame = 0
                    v1 = s_dve.inc(nc.vector.tensor_tensor(kbT[:, :], blkK[:, csl], pat_kb[:, :], OP.mult))
                    nc.tensor.wait_ge(s_dve.h, max(v1, ptfree))
                    for p in range(8):
                        ti = nc.tensor.transpose(pT[:, p * 128:(p + 1) * 128], kbT[:, p * 128:(p + 1) * 128], ident[:, :])
                    tv = s_te.inc(ti)
                    nc.vector.wait_ge(s_te.h, tv)
                    kbe = s_dve.inc(nc.vector.tensor_copy(kbtok[:, :], pT[:, :]))
                    ptfree = kbe
                    if heavy and hs >= 1:
                        nc.vector.wait_ge(s_sce.h, max(sfree))
                        nc.vector.tensor_tensor(r3(qt8[:, :], 8).rearrange("p a (b q) -> p a b q", b=2),
                                                r3(blkQ[:, csl], 8).rearrange("p a (b q) -> p a b q", b=2),
                                                pat_q1_4, OP.mult)
                        qt3 = r3(qt8[:, :], 8)
                        nc.vector.tensor_tensor(r3(qh8[:, :], 8), qt3[:, :, 64:128], inva3, OP.mult)
                        nc.vector.tensor_tensor(r3(kt8[:, :], 8).rearrange("p a (b q) -> p a b q", b=2),
                                                r3(blkK[:, csl], 8).rearrange("p a (b q) -> p a b q", b=2),
                                                pat_k_4, OP.mult)
                        kt3 = r3(kt8[:, :], 8)
                        nc.vector.tensor_tensor(r3(kh8[:, :], 8), kt3[:, :, 0:64], al653, OP.mult)
                        nc.vector.tensor_copy(Hb[:, :], Hf[:, :])
                        qk_done = s_dve.inc(nc.vector.tensor_tensor(H64b[:, :], Hf[:, :], al643, OP.mult))
                        ame = 0
                        for g in range(4 if hs >= 2 else 0):
                            p0, p1 = 2 * g, 2 * g + 1
                            nc.tensor.wait_ge(s_dve.h, max(qk_done, ame, dfree))
                            for gi_, p in enumerate((p0, p1)):
                                for hh, ro in ((0, 0), (1, 64)):
                                    bank = pb[2] if hh == 0 else pb[5]
                                    co = gi_ * 128
                                    ksl = slice(ro, ro + 64)
                                    nc.tensor.matmul(bank[0:64, co + 0:co + 64],
                                                     kt8[ksl, p * 128:p * 128 + 64], qt8[ksl, p * 128:p * 128 + 64],
                                                     start=True, stop=True, tile_position=(ro, 0))
                                    nc.tensor.matmul(bank[64:128, co + 64:co + 128],
                                                     kt8[ksl, p * 128 + 64:p * 128 + 128], qt8[ksl, p * 128 + 64:p * 128 + 128],
                                                     start=True, stop=True, tile_position=(ro, 64))
                                    am = nc.tensor.matmul(bank[0:64, co + 64:co + 128],
                                                          kh8[ksl, p * 64:p * 64 + 64], qh8[ksl, p * 64:p * 64 + 64],
                                                          start=True, stop=True, tile_position=(ro, 0))
                            atv = s_te.inc(am)
                            nc.vector.wait_ge(s_te.h, atv)
                            nc.vector.tensor_tensor(
                                r3(Am[:, 0:256], 2), r3(pb[2][:, 0:256], 2), mask2, OP.mult)
                            ame = s_dve.inc(nc.vector.tensor_tensor(
                                r3(Am[:, 256:512], 2), r3(pb[5][:, 0:256], 2), mask2, OP.mult))
                            for gi_, p in enumerate((p0, p1) if hs >= 3 else ()):
                                ob = pb[3] if p % 2 == 0 else pb[4]
                                abase = gi_ * 128
                                nc.tensor.wait_ge(s_dve.h, max(ame, ogfree[p % 2]))
                                nc.tensor.matmul(ob[0:64, 0:128], blkV[:, c * 1024 + (2 * p) * 64: c * 1024 + (2 * p) * 64 + 64],
                                                 Am[:, abase:abase + 128], start=True, stop=False, tile_position=(0, 0))
                                nc.tensor.matmul(ob[0:64, 0:64], Hb[0:64, p * 64:p * 64 + 64],
                                                 qt8[0:64, p * 128:p * 128 + 64], start=False, stop=False, tile_position=(0, 0))
                                nc.tensor.matmul(ob[0:64, 64:128], H64b[0:64, p * 64:p * 64 + 64],
                                                 qt8[0:64, p * 128 + 64:p * 128 + 128], start=False, stop=False, tile_position=(0, 0))
                                nc.tensor.matmul(ob[64:128, 0:128], blkV[:, c * 1024 + (2 * p + 1) * 64: c * 1024 + (2 * p + 1) * 64 + 64],
                                                 Am[:, 256 + abase:256 + abase + 128], start=True, stop=False, tile_position=(0, 64))
                                nc.tensor.matmul(ob[64:128, 0:64], Hb[64:128, p * 64:p * 64 + 64],
                                                 qt8[64:128, p * 128:p * 128 + 64], start=False, stop=False, tile_position=(64, 64))
                                oi = nc.tensor.matmul(ob[64:128, 64:128], H64b[64:128, p * 64:p * 64 + 64],
                                                      qt8[64:128, p * 128 + 64:p * 128 + 128], start=False, stop=True, tile_position=(64, 64))
                                otv = s_te.inc(oi)
                                nc.vector.wait_ge(s_te.h, otv)
                                nc.vector.wait_ge(s_sce.h, max(sfree))
                                ogv = s_dve.inc(nc.vector.tensor_tensor(
                                    og8[:, p * 128:(p + 1) * 128], ob[:, 0:128],
                                    blkG[:, c * 1024 + p * 128: c * 1024 + (p + 1) * 128], OP.mult))
                                ogfree[p % 2] = ogv
                        gguard = s_dve.n
                    nc.tensor.wait_ge(s_dve.h, max(kbe, dfree, ame))
                    for p in range(8):
                        nc.tensor.matmul(pb[5][0:64, p * 64:(p + 1) * 64], kbtok[:, p * 128:p * 128 + 64],
                                         blkV[:, c * 1024 + (2 * p) * 64: c * 1024 + (2 * p) * 64 + 64],
                                         start=True, stop=True, tile_position=(0, 0))
                        di = nc.tensor.matmul(pb[5][64:128, p * 64:(p + 1) * 64], kbtok[:, p * 128 + 64:p * 128 + 128],
                                              blkV[:, c * 1024 + (2 * p + 1) * 64: c * 1024 + (2 * p + 1) * 64 + 64],
                                              start=True, stop=True, tile_position=(0, 64))
                    dtv = s_te.inc(di)
                    nc.vector.wait_ge(s_te.h, dtv)
                    nc.vector.tensor_tensor(r3(Hf[:, :], 8), r3(Hf[:, :], 8), alC3, OP.mult)
                    dfree = s_dve.inc(nc.vector.tensor_tensor(Hf[:, :], Hf[:, :], pb[5][:, :], OP.add))
                    if heavy and hs >= 4:
                        oc = 2 * (b - 8) + c
                        nc.sync.wait_ge(s_dve.h, xfree)
                        xl = s_ld.inc(nc.sync.dma_start(xtile[:, :], x_own[oc * 128:(oc + 1) * 128, :]), 16)
                        for hh in range(2):
                            nc.tensor.wait_ge(s_dve.h, max(yfree, gguard))
                            for p in range(8):
                                nc.tensor.matmul(pb[6][:, :], og8[:, p * 128:(p + 1) * 128],
                                                 w_attn["wo"][:, p * D + hh * 512: p * D + hh * 512 + 512],
                                                 start=(p == 0), stop=False)
                            yi = nc.tensor.matmul(pb[6][:, :], onesb[0:1, 0:128], bobp[0:1, hh * 512:hh * 512 + 512],
                                                  start=False, stop=True)
                            ytv = s_te.inc(yi)
                            nc.vector.wait_ge(s_te.h, ytv)
                            nc.vector.wait_ge(s_ld.h, xl)
                            nc.vector.wait_ge(s_st.h, ystfree[hh])
                            nc.vector.tensor_scalar(xtile[:, hh * 512:(hh + 1) * 512], xtile[:, hh * 512:(hh + 1) * 512],
                                                    stats_s1[:, oc:oc + 1], stats_s2[:, oc:oc + 1], OP.mult, OP.subtract)
                            yev = s_dve.inc(nc.vector.tensor_tensor(yst[hh][:, :], xtile[:, hh * 512:(hh + 1) * 512],
                                                                    pb[6][:, :], OP.add))
                            yfree = yev
                            if hh == 1:
                                xfree = yev
                            nc.sync.wait_ge(s_dve.h, yev)
                            ystfree[hh] = s_st.inc(nc.sync.dma_start(
                                ya_dram[oc * 128:(oc + 1) * 128, hh * 512:(hh + 1) * 512], yst[hh][:, :]), 16)
                if b == 7:
                    nc.vector.wait_ge(s_dve.h, s_dve.n)
                    s_dve.inc(nc.vector.tensor_scalar_mul(Hf[:, :], Hf[:, :], csc[:, 0:1]))
            for nm, buf in [("blkK", blkK), ("blkV", blkV), ("blkQ", blkQ), ("blkG", blkG),
                            ("og8", og8), ("Hf", Hf), ("qt8", qt8), ("kt8", kt8),
                            ("kbtok", kbtok), ("Am", Am[:, 0:512])]:
                if nm in tap_aps:
                    barrier()
                    s_st.inc(nc.sync.dma_start(tap_aps[nm][:, :], buf if nm in ("Am",) else buf[:, :]), 16)
            nc.sync.wait_ge(s_st.h, s_st.n)

        if "ya" in tap_aps:
            barrier()
            s_st.inc(nc.sync.dma_start(tap_aps["ya"][:, :], ya_dram[:, :]), 16)
        if "xn2T" in tap_aps:
            barrier()
            s_st.inc(nc.sync.dma_start(tap_aps["xn2T"][:, :], xn2T[:, :]), 16)
        nc.sync.wait_ge(s_st.h, s_st.n)
        barrier()
        es12.close()
        if LV < 4:
            _finish(nc, s_st, s_dve, s_sce, s_te, tap_aps, s_ld)
            return nc, tap_aps

        barrier()
        es3 = ExitStack()
        with es3:
            sb3 = lambda name, shape, dtp: es3.enter_context(nc.sbuf_tensor(name, shape, dtp))
            es3kv = ExitStack()
            sb3kv = lambda name, shape, dtp: es3kv.enter_context(nc.sbuf_tensor(name, shape, dtp))
            fkT = sb3kv("fkT", [128, 8 * T], BF16)
            fvT = sb3kv("fvT", [128, 32 * 1024], BF16)
            fqT = sb3kv("fqT", [128, 8 * HALF], BF16)
            with ExitStack() as es3a:
                wbuf = es3a.enter_context(nc.sbuf_tensor("wbuf", [128, 8 * D], BF16))
                xnTb2 = [es3a.enter_context(nc.sbuf_tensor(f"xnTb{i}", [128, 8 * 512], BF16)) for i in range(2)]
                pfree = [0, 0]
                wfree = s_te.n
                xg = {0: 0, 1: 0}
                for wnm, kind in [("fk", "k"), ("fv", "v"), ("fq", "q")]:
                    nc.gpsimd.wait_ge(s_te.h, wfree)
                    wl = s_ldw.inc(nc.gpsimd.dma_start(r3(wbuf[:, :], 8),
                                                       wts[wnm][:, :].rearrange("(i p) o -> p i o", p=128)), 16)
                    blks = range(8) if kind in ("k", "v") else range(4, 8)
                    for bix, blk in enumerate(blks):
                        xnTb = xnTb2[bix % 2]
                        nc.sync.wait_ge(s_te.h, xg[bix % 2])
                        xv = s_ld.inc(nc.sync.dma_start(
                            r3(xnTb[:, :], 8), r3(xnT_dram[:, :], 8)[:, :, blk * 512:(blk + 1) * 512]), 16)
                        nc.tensor.wait_ge(s_ld.h, xv)
                        nc.tensor.wait_ge(s_ldw.h, wl)
                        if kind in ("k", "q"):
                            bc = bcol_f32["fbk" if kind == "k" else "fbq"]
                            for p in range(8):
                                bank = pb[p % 2]
                                nc.tensor.wait_ge(s_dve.h, pfree[p % 2])
                                for i in range(8):
                                    mi = nc.tensor.matmul(bank[:, 0:512],
                                                          wbuf[:, i * D + p * 128: i * D + (p + 1) * 128],
                                                          xnTb[:, i * 512:(i + 1) * 512], start=(i == 0), stop=(i == 7))
                                tv = s_te.inc(mi)
                                nc.vector.wait_ge(s_te.h, tv)
                                if kind == "k":
                                    dst = fkT[:, p * T + blk * 512: p * T + (blk + 1) * 512]
                                else:
                                    dst = fqT[:, p * HALF + (blk - 4) * 512: p * HALF + (blk - 3) * 512]
                                pfree[p % 2] = s_dve.inc(nc.vector.tensor_scalar(dst, bank[:, 0:512],
                                                                                 bc[:, p:p + 1], None, OP.add))
                            xg[bix % 2] = s_te.n
                        else:
                            for tc in range(4):
                                for fh in range(2):
                                    bk_ = (tc * 2 + fh) % 2
                                    bank = pb[bk_]
                                    nc.tensor.wait_ge(s_dve.h, pfree[bk_])
                                    for i in range(8):
                                        nc.tensor.matmul(bank[:, 0:512],
                                                         xnTb[:, i * 512 + tc * 128: i * 512 + (tc + 1) * 128],
                                                         wbuf[:, i * D + fh * 512: i * D + fh * 512 + 512],
                                                         start=(i == 0), stop=False)
                                    mi = nc.tensor.matmul(bank[:, 0:512], onesb[0:1, 0:128],
                                                          fbvrow[0:1, fh * 512:fh * 512 + 512], start=False, stop=True)
                                    tv = s_te.inc(mi)
                                    nc.vector.wait_ge(s_te.h, tv)
                                    pfree[bk_] = s_dve.inc(nc.vector.tensor_copy(
                                        fvT[:, (blk * 4 + tc) * 1024 + fh * 512: (blk * 4 + tc) * 1024 + fh * 512 + 512],
                                        bank[:, 0:512]))
                        xg[bix % 2] = s_te.n
                    wfree = s_te.n
                nc.gpsimd.wait_ge(s_dve.h, s_dve.n)

            if "fkT" in tap_aps:
                barrier()
                s_st.inc(nc.sync.dma_start(tap_aps["fkT"][:, :], fkT[:, :]), 16)
            if "fvT" in tap_aps:
                barrier()
                s_st.inc(nc.sync.dma_start(tap_aps["fvT"][:, :], fvT[:, :]), 16)
            if "fqT" in tap_aps:
                barrier()
                s_st.inc(nc.sync.dma_start(tap_aps["fqT"][:, :], fqT[:, :]), 16)

            if LV < 5:
                barrier()
                es3kv.close()
                _finish(nc, s_st, s_dve, s_sce, s_te, tap_aps, s_ld)
                return nc, tap_aps
            barrier()
            with ExitStack() as es3b:
                sb3b = lambda name, shape, dtp: es3b.enter_context(nc.sbuf_tensor(name, shape, dtp))
                qs = sb3b("qs", [128, 512], BF16)
                qsq = sb3b("qsq", [128, 512], BF16)
                sr = sb3b("sr", [128, 512], F32)
                pt = [sb3b(f"pt{i}", [128, 1024], BF16) for i in range(3)]
                kmx = sb3b("kmx", [128, 8], F32)
                hst = [sb3b(f"hst{i}", [128, 512], BF16) for i in range(2)]
                hstfree = [0, 0]
                hsti = 0
                qesq = sb3b("qesq", [128, 8], F32)
                rowb = sb3b("rowb", [1, 32], F32)
                bcst = sb3b("bcst", [128, 512], BF16)

                for p in range(8):
                    kv = s_dve.inc(nc.vector.reduce_max(kmx[:, p:p + 1], fkT[:, p * T:(p + 1) * T], axis=AX,
                                                        apply_absolute_value=True))
                nc.vector.wait_ge(s_dve.h, kv)
                kv = s_dve.inc(nc.vector.tensor_tensor(kmx[:, :], kmx[:, :], kmx[:, :], OP.mult))
                qev = s_dve.inc(nc.vector.tensor_tensor(qesq[:, :], qecol[:, :], qecol[:, :], OP.mult))
                nc.tensor.wait_ge(s_dve.h, max(kv, qev))
                for hh in range(2):
                    rs_ = slice(64 * hh, 64 * hh + 64)
                    for p in range(8):
                        h = 2 * p + hh
                        nc.tensor.matmul(pb[0][0:1, h:h + 1], onesf[rs_, 0:1], kmx[rs_, p:p + 1],
                                         start=True, stop=True, tile_position=(64 * hh, 0))
                        mi = nc.tensor.matmul(pb[0][0:1, 16 + h:16 + h + 1], onesf[rs_, 0:1], qesq[rs_, p:p + 1],
                                              start=True, stop=True, tile_position=(64 * hh, 0))
                    sv_ = s_te.inc(mi)
                    nc.tensor.wait_ge(s_te.h, sv_)
                bv_ = sv_
                nc.scalar.wait_ge(s_te.h, bv_)
                v_ = s_sce.inc(nc.scalar.activation(pb[0][0:1, 0:32], pb[0][0:1, 0:32], AF.Ln))
                nc.scalar.wait_ge(s_sce.h, v_)
                v_ = s_sce.inc(nc.scalar.activation(pb[0][0:1, 0:32], pb[0][0:1, 0:32], AF.Exp, scale=0.5))
                nc.vector.wait_ge(s_sce.h, v_)
                v_ = s_dve.inc(nc.vector.tensor_copy(rowb[0:1, 0:32], pb[0][0:1, 0:32]))
                nc.vector.wait_ge(s_dve.h, v_)
                v_ = s_dve.inc(nc.vector.tensor_scalar_add(rowb[0:1, 16:32], rowb[0:1, 16:32], 1.0))
                nc.vector.wait_ge(s_dve.h, v_)
                v_ = s_dve.inc(nc.vector.tensor_tensor(rowb[0:1, 0:16], rowb[0:1, 0:16], rowb[0:1, 16:32], OP.mult))
                nc.vector.wait_ge(s_dve.h, v_)
                v_ = s_dve.inc(nc.vector.tensor_tensor(rowb[0:1, 0:16], rowb[0:1, 0:16], sp8[0:1, :], OP.mult))
                nc.vector.wait_ge(s_dve.h, v_)
                v_ = s_dve.inc(nc.vector.tensor_scalar_mul(rowb[0:1, 0:16], rowb[0:1, 0:16], -1.0))
                nc.tensor.wait_ge(s_dve.h, v_)
                mi = nc.tensor.matmul(pb[1][:, 0:16], onesf[0:1, 0:128], rowb[0:1, 0:16], start=True, stop=True)
                bv_ = s_te.inc(mi)
                nc.vector.wait_ge(s_te.h, bv_)
                v_ = s_dve.inc(nc.vector.tensor_copy(negM[:, :], pb[1][:, 0:16]))
                nc.vector.wait_ge(s_dve.h, v_)
                nm_v = s_dve.inc(nc.vector.tensor_tensor(
                    negMp[:, :], negM[:, 0:16].rearrange("p (a b) -> p a b", b=2)[:, :, 0:1].squeeze(2),
                    negM[:, 0:16].rearrange("p (a b) -> p a b", b=2)[:, :, 1:2].squeeze(2), OP.min))
                if "negM" in tap_aps:
                    nc.sync.wait_ge(s_dve.h, nm_v)
                    s_st.inc(nc.sync.dma_start(tap_aps["negM"][:, :], negM[:, :]), 16)
                nc.tensor.wait_ge(s_dve.h, nm_v)

                ppb = [0, 0]
                nqv = 0
                for p in range(8):
                    for qb in range(4):
                        qsl = slice(p * HALF + qb * 512, p * HALF + (qb + 1) * 512)
                        bk_ = (p * 4 + qb) % 2
                        nc.vector.wait_ge(s_te.h, nqv)
                        qv = s_dve.inc(nc.vector.tensor_tensor(qsq[:, :], fqT[:, qsl], fqT[:, qsl], OP.mult))
                        nc.tensor.wait_ge(s_dve.h, max(qv, ppb[bk_]))
                        nc.tensor.matmul(pb[3 * bk_][0:1, :], onesb[0:64, 0:1], qsq[0:64, :],
                                         start=True, stop=True, tile_position=(0, 0))
                        nv = s_te.inc(nc.tensor.matmul(pb[3 * bk_][64:65, :], onesb[64:128, 0:1], qsq[64:128, :],
                                                       start=True, stop=True, tile_position=(64, 64)))
                        nqv = nv
                        nc.scalar.wait_ge(s_te.h, nv)
                        for ro in (0, 64):
                            v_ = s_sce.inc(nc.scalar.activation(sr[ro:ro + 1, :], pb[3 * bk_][ro:ro + 1, :], AF.Ln))
                            nc.scalar.wait_ge(s_sce.h, v_)
                            v_ = s_sce.inc(nc.scalar.activation(sr[ro:ro + 1, :], sr[ro:ro + 1, :], AF.Exp, scale=-0.5))
                        rsv = v_
                        nc.tensor.wait_ge(s_sce.h, rsv)
                        nc.tensor.matmul(pb[1 + 3 * bk_][0:64, :], onesf[0:1, 0:64], sr[0:1, :], start=True, stop=True,
                                         tile_position=(0, 0))
                        bc = s_te.inc(nc.tensor.matmul(pb[1 + 3 * bk_][64:128, :], onesf[64:65, 0:64], sr[64:65, :],
                                                       start=True, stop=True, tile_position=(64, 64)))
                        nc.vector.wait_ge(s_te.h, bc)
                        v_ = s_dve.inc(nc.vector.tensor_tensor(qs[:, :], fqT[:, qsl], pb[1 + 3 * bk_][:, :], OP.mult))
                        nc.vector.wait_ge(s_dve.h, v_)
                        v_ = s_dve.inc(nc.vector.tensor_scalar(qs[:, :], qs[:, :], qecol[:, p:p + 1], None, OP.add))
                        nc.vector.wait_ge(s_dve.h, v_)
                        ppb[bk_] = s_dve.inc(nc.vector.tensor_scalar_mul(fqT[:, qsl], qs[:, :], spp[:, p:p + 1]))
                nc.tensor.wait_ge(s_dve.h, max(ppb))

                exp_done = {0: s_sce.n, 1: s_sce.n}
                avdone = {0: s_te.n, 1: s_te.n, 2: s_te.n}
                sv = {0: 0, 1: 0}
                hvfree = 0
                bcstv = 0
                for p in range(8):
                    for qb in range(4):
                        qsl = slice(p * HALF + qb * 512, p * HALF + (qb + 1) * 512)
                        for kc in range(33):
                            par = kc % 2
                            if kc < 32:
                                nc.tensor.wait_ge(s_sce.h, exp_done[par])
                                ksl = slice(p * T + kc * 128, p * T + (kc + 1) * 128)
                                nc.tensor.matmul(pb[2 * par][:, :], fkT[0:64, ksl], fqT[0:64, qsl], start=True, stop=True,
                                                 tile_position=(0, 0))
                                sv[par] = s_te.inc(nc.tensor.matmul(pb[2 * par + 1][:, :], fkT[64:128, ksl], fqT[64:128, qsl],
                                                                    start=True, stop=True, tile_position=(64, 0)))
                            if kc >= 1:
                                q_ = kc - 1
                                parq = q_ % 2
                                pt3 = q_ % 3
                                nc.scalar.wait_ge(s_te.h, sv[parq])
                                nc.scalar.wait_ge(s_te.h, avdone[pt3])
                                exp_done[parq] = s_sce.inc(nc.scalar.activation(
                                    pt[pt3][:, :], S2[parq].ap()[:, :], AF.Exp, bias=negMp[:, p:p + 1]))
                                nc.tensor.wait_ge(s_sce.h, exp_done[parq])
                                if q_ == 0:
                                    nc.tensor.wait_ge(s_dve.h, max(hvfree, bcstv))
                                vsl0 = slice(q_ * 1024 + (2 * p) * 64, q_ * 1024 + (2 * p) * 64 + 64)
                                vsl1 = slice(q_ * 1024 + (2 * p + 1) * 64, q_ * 1024 + (2 * p + 1) * 64 + 64)
                                nc.tensor.matmul(pb[4][0:64, :], fvT[:, vsl0], pt[pt3][:, 0:512],
                                                 start=(q_ == 0), stop=(q_ == 31), tile_position=(0, 0))
                                nc.tensor.matmul(pb[5][64:128, :], fvT[:, vsl1], pt[pt3][:, 512:1024],
                                                 start=(q_ == 0), stop=(q_ == 31), tile_position=(0, 64))
                                nc.tensor.matmul(pb[6][0:1, :], onesb[:, 0:1], pt[pt3][:, 0:512],
                                                 start=(q_ == 0), stop=(q_ == 31), tile_position=(0, 0))
                                av = nc.tensor.matmul(pb[6][64:65, :], onesb[:, 0:1], pt[pt3][:, 512:1024],
                                                      start=(q_ == 0), stop=(q_ == 31), tile_position=(0, 64))
                                avdone[pt3] = s_te.inc(av)
                        nc.vector.wait_ge(s_te.h, max(avdone.values()))
                        v_ = s_dve.inc(nc.vector.reciprocal(sr[0:1, :], pb[6][0:1, :]))
                        rv = s_dve.inc(nc.vector.reciprocal(sr[64:65, :], pb[6][64:65, :]))
                        nc.tensor.wait_ge(s_dve.h, rv)
                        nc.tensor.matmul(pb[6][0:64, :], onesf[0:1, 0:64], sr[0:1, :], start=True, stop=True,
                                         tile_position=(0, 0))
                        bc = s_te.inc(nc.tensor.matmul(pb[6][64:128, :], onesf[64:65, 0:64], sr[64:65, :],
                                                       start=True, stop=True, tile_position=(64, 64)))
                        nc.vector.wait_ge(s_te.h, bc)
                        nc.vector.wait_ge(s_st.h, hstfree[hsti])
                        bcstv = s_dve.inc(nc.vector.tensor_copy(bcst[:, :], pb[6][:, :]))
                        nc.vector.tensor_tensor(hst[hsti][0:64, :], pb[4][0:64, :], bcst[0:64, :], OP.mult)
                        hvfree = s_dve.inc(nc.vector.tensor_tensor(hst[hsti][64:128, :], pb[5][64:128, :],
                                                                   bcst[64:128, :], OP.mult))
                        nc.sync.wait_ge(s_dve.h, hvfree)
                        hstfree[hsti] = s_st.inc(nc.sync.dma_start(hT_dram[:, qsl], hst[hsti][:, :]), 16)
                        hsti ^= 1

            barrier()
            es3kv.close()
            if LV < 6:
                if "hT" in tap_aps and LV >= 5:
                    s_st.inc(nc.sync.dma_start(tap_aps["hT"][:, :], hT_dram[:, :]), 16)
                _finish(nc, s_st, s_dve, s_sce, s_te, tap_aps, s_ld)
                return nc, tap_aps
            if "hT" in tap_aps:
                s_st.inc(nc.sync.dma_start(tap_aps["hT"][:, :], hT_dram[:, :]), 16)
                nc.sync.wait_ge(s_st.h, s_st.n)

            barrier()
            with ExitStack() as es3c:
                sb3c = lambda name, shape, dtp: es3c.enter_context(nc.sbuf_tensor(name, shape, dtp))
                fpw = sb3c("fpw", [128, 8 * D], BF16)
                hTt = [sb3c(f"hTt{i}", [128, 1024], BF16) for i in range(2)]
                hTtfree = [0, 0]
                wrs = sb3c("wrs", [128, 8 * 17], BF16)
                lg = sb3c("lg", [128, 24], F32)
                m8 = sb3c("m8", [128, 8], F32)
                yac = [sb3c(f"yac{i}", [128, 512], F32) for i in range(2)]
                outb = [sb3c(f"outb{i}", [128, 512], F32) for i in range(2)]
                MX15, S15, RC15, T3, MX2, SW0, SW1, GTE = 15, 16, 17, 18, 19, 20, 21, 22

                wv1 = s_ldw.inc(nc.gpsimd.dma_start(r3(fpw[:, :], 8),
                                                    wts["fp"][:, :].rearrange("(i p) o -> p i o", p=128)), 16)
                wv2 = s_ldw.inc(nc.gpsimd.dma_start(
                    r3(wrs[:, :], 8)[:, :, 0:15], wr_t[:, :].rearrange("(i p) o -> p i o", p=128)), 16)
                wv2 = s_ldw.inc(nc.gpsimd.dma_start(
                    r3(wrs[:, :], 8)[:, :, 15:17], ws_t[:, :].rearrange("(i p) o -> p i o", p=128)), 16)
                yfree2 = 0
                pfree2 = 0
                yacfree = [0, 0]
                outfree = [0, 0]
                grfree = 0
                for tc in range(16):
                    ht = hTt[tc % 2]
                    nc.sync.wait_ge(s_te.h, hTtfree[tc % 2])
                    hl = s_ld.inc(nc.sync.dma_start(
                        r3(ht[:, :], 8), r3(hT_dram[:, :], 8)[:, :, tc * 128:(tc + 1) * 128]), 16)
                    nc.tensor.wait_ge(s_ldw.h, wv2)
                    nc.tensor.wait_ge(s_dve.h, pfree2)
                    nc.tensor.wait_ge(s_ld.h, hl)
                    for i in range(8):
                        nc.tensor.matmul(pb[0][:, 0:17], ht[:, i * 128:(i + 1) * 128],
                                         wrs[:, i * 17:(i + 1) * 17], start=(i == 0), stop=False)
                    li = nc.tensor.matmul(pb[0][:, 0:17], onesb[0:1, 0:128], brsrow[0:1, :], start=False, stop=True)
                    lv = s_te.inc(li)
                    nc.vector.wait_ge(s_te.h, lv)
                    pfree2 = s_dve.inc(nc.vector.tensor_copy(lg[:, 0:17], pb[0][:, 0:17]))
                    nc.vector.wait_ge(s_dve.h, pfree2)
                    v_ = s_dve.inc(nc.vector.reduce_max(lg[:, MX15:MX15 + 1], lg[:, 0:15], axis=AX, negate=True))
                    nc.vector.wait_ge(s_dve.h, v_)
                    v2_ = s_dve.inc(nc.vector.reduce_max(lg[:, MX2:MX2 + 1], lg[:, 15:17], axis=AX, negate=True))
                    nc.scalar.wait_ge(s_dve.h, v2_)
                    e_ = s_sce.inc(nc.scalar.activation(lg[:, 0:15], lg[:, 0:15], AF.Exp, bias=lg[:, MX15:MX15 + 1]))
                    e2_ = s_sce.inc(nc.scalar.activation(lg[:, 15:17], lg[:, 15:17], AF.Exp, bias=lg[:, MX2:MX2 + 1]))
                    nc.vector.wait_ge(s_sce.h, e2_)
                    v_ = s_dve.inc(nc.vector.reduce_sum(lg[:, S15:S15 + 1], lg[:, 0:15], axis=AX))
                    nc.vector.wait_ge(s_dve.h, v_)
                    v_ = s_dve.inc(nc.vector.reciprocal(lg[:, RC15:RC15 + 1], lg[:, S15:S15 + 1]))
                    nc.vector.wait_ge(s_dve.h, v_)
                    v_ = s_dve.inc(nc.vector.max(out=m8[:, :], in_=lg[:, 0:15]))
                    nc.vector.wait_ge(s_dve.h, v_)
                    v_ = s_dve.inc(nc.vector.tensor_tensor(lg[:, T3:T3 + 1], m8[:, 0:1], m8[:, 1:2], OP.add))
                    nc.vector.wait_ge(s_dve.h, v_)
                    v_ = s_dve.inc(nc.vector.tensor_tensor(lg[:, T3:T3 + 1], lg[:, T3:T3 + 1], m8[:, 2:3], OP.add))
                    nc.vector.wait_ge(s_dve.h, v_)
                    v_ = s_dve.inc(nc.vector.tensor_tensor(lg[:, T3:T3 + 1], lg[:, T3:T3 + 1], lg[:, RC15:RC15 + 1], OP.mult))
                    v_ = s_dve.inc(nc.vector.tensor_tensor(lg[:, SW0:SW0 + 1], lg[:, 15:16], lg[:, 16:17], OP.add))
                    nc.vector.wait_ge(s_dve.h, v_)
                    v_ = s_dve.inc(nc.vector.reciprocal(lg[:, SW1:SW1 + 1], lg[:, SW0:SW0 + 1]))
                    nc.vector.wait_ge(s_dve.h, v_)
                    v_ = s_dve.inc(nc.vector.tensor_tensor(lg[:, SW0:SW0 + 1], lg[:, 15:16], lg[:, SW1:SW1 + 1], OP.mult))
                    nc.vector.wait_ge(s_dve.h, v_)
                    v_ = s_dve.inc(nc.vector.tensor_tensor(lg[:, SW1:SW1 + 1], lg[:, 16:17], lg[:, SW1:SW1 + 1], OP.mult))
                    nc.vector.wait_ge(s_dve.h, v_)
                    v_ = s_dve.inc(nc.vector.tensor_tensor(lg[:, GTE:GTE + 1], lg[:, SW1:SW1 + 1], lg[:, T3:T3 + 1], OP.mult))
                    nc.vector.wait_ge(s_dve.h, v_)
                    v_ = s_dve.inc(nc.vector.tensor_scalar(lg[:, GTE:GTE + 1], lg[:, GTE:GTE + 1], 3.0, None, OP.mult))
                    nc.vector.wait_ge(s_dve.h, v_)
                    v_ = s_dve.inc(nc.vector.tensor_tensor(lg[:, GTE:GTE + 1], lg[:, GTE:GTE + 1], lg[:, SW0:SW0 + 1], OP.add))
                    nc.vector.wait_ge(s_dve.h, v_)
                    s_dve.inc(nc.vector.tensor_scalar(lg[:, 23:24], lg[:, GTE:GTE + 1], 2.0, None, OP.mult))
                    for hh in range(2):
                        nc.sync.wait_ge(s_dve.h, yacfree[hh])
                        ya_v = s_ld.inc(nc.sync.dma_start(yac[hh][:, :],
                                                          ya_dram[tc * 128:(tc + 1) * 128, hh * 512:(hh + 1) * 512]), 16)
                        nc.tensor.wait_ge(s_dve.h, yfree2)
                        nc.tensor.wait_ge(s_ldw.h, wv1)
                        for i in range(8):
                            yi = nc.tensor.matmul(pb[2][:, :], ht[:, i * 128:(i + 1) * 128],
                                                  fpw[:, i * D + hh * 512: i * D + hh * 512 + 512],
                                                  start=(i == 0), stop=(i == 7))
                        ytv = s_te.inc(yi)
                        nc.vector.wait_ge(s_te.h, ytv)
                        nc.vector.wait_ge(s_ld.h, ya_v)
                        nc.vector.wait_ge(s_st.h, outfree[hh])
                        yev = s_dve.inc(nc.vector.scalar_tensor_tensor(outb[hh][:, :], pb[2][:, :], lg[:, 23:24],
                                                                       yac[hh][:, :], OP.mult, OP.add))
                        yfree2 = yev
                        yacfree[hh] = yev
                        nc.sync.wait_ge(s_dve.h, yev)
                        outfree[hh] = s_st.inc(nc.sync.dma_start(
                            out[tc * 128:(tc + 1) * 128, hh * 512:(hh + 1) * 512], outb[hh][:, :]), 16)
                    hTtfree[tc % 2] = s_te.n

        if "stats" in tap_aps:
            s_st.inc(nc.sync.dma_start(tap_aps["stats"][:, 0:16], stats_s1[:, :]), 16)
            s_st.inc(nc.sync.dma_start(tap_aps["stats"][:, 16:32], stats_s2[:, :]), 16)
        if "xnT" in tap_aps:
            s_st.inc(nc.sync.dma_start(tap_aps["xnT"][:, :], xnT_dram[:, :]), 16)
        nc.sync.wait_ge(s_st.h, s_st.n)

    return nc, tap_aps



def _prep_shared(inputs):
    f = np.float32
    g = {}
    tr = lambda a: np.ascontiguousarray(np.asarray(a, f).T)
    g["wq_t"] = tr(inputs["L0_attn_WQ"]); g["wk_t"] = tr(inputs["L0_attn_WK"])
    g["wv_t"] = tr(inputs["L0_attn_WV"]); g["wg_t"] = tr(inputs["L0_attn_Wg"])
    g["wo_t"] = tr(inputs["L0_attn_WO"])
    g["fq_t"] = tr(inputs["L0_ffn_Wq"]); g["fk_t"] = tr(inputs["L0_ffn_Wk"])
    g["fv_t"] = tr(inputs["L0_ffn_Wv"]); g["fp_t"] = tr(inputs["L0_ffn_Wp"])
    g["wr_t"] = tr(inputs["L0_ffn_Wr"]); g["ws_t"] = tr(inputs["L0_ffn_Ws"])
    col = lambda b: np.ascontiguousarray(np.asarray(b, f).reshape(8, 128).T)
    g["bq_c"] = col(inputs["L0_attn_bQ"]); g["bk_c"] = col(inputs["L0_attn_bK"])
    g["bg_c"] = col(inputs["L0_attn_bg"])
    g["fbq_c"] = col(inputs["L0_ffn_bq"]); g["fbk_c"] = col(inputs["L0_ffn_bk"])
    g["bv_att_row"] = np.asarray(inputs["L0_attn_bV"], f).reshape(1, D).copy()
    g["fbv_row"] = np.asarray(inputs["L0_ffn_bv"], f).reshape(1, D).copy()
    g["bo_row"] = np.asarray(inputs["L0_attn_bO"], f).reshape(1, D).copy()
    g["bp_row"] = np.asarray(inputs["L0_ffn_bp"], f).reshape(1, D).copy()
    g["brs_row"] = np.concatenate([np.asarray(inputs["L0_ffn_br"], f).ravel(),
                                   np.asarray(inputs["L0_ffn_bs"], f).ravel()]).reshape(1, 17).copy()
    g["gamma_c"] = col(inputs["L0_attn_ln_gamma"]); g["beta_c"] = col(inputs["L0_attn_ln_beta"])
    pairstack = lambda a: np.ascontiguousarray(
        np.transpose(np.asarray(a, f).reshape(8, 2, 64), (1, 2, 0)).reshape(128, 8))
    g["al_t2"] = pairstack(inputs["L0_attn_alpha_logit"])
    g["qe_t2"] = pairstack(np.asarray(inputs["L0_ffn_query_embedding"], f)[:, 0, :])
    g["temp_row"] = np.asarray(inputs["L0_ffn_temperature"], f).reshape(1, 16).copy()
    return g


def make_in_maps(inputs):
    shared = _prep_shared(inputs)
    x = np.asarray(inputs["L0_attn_x"], np.float32)
    in_maps = []
    for b in range(B):
        for j in range(2):
            m = dict(shared)
            m["x_own"] = np.ascontiguousarray(x[b, j * HALF:(j + 1) * HALF])
            m["x_other"] = np.ascontiguousarray(x[b, (1 - j) * HALF:(2 - j) * HALF])
            m["chain_scale"] = np.full((128, 1), float(j), np.float32)
            in_maps.append(m)
    return in_maps


def kernel(**inputs):
    nc, _ = build_nc()
    in_maps = make_in_maps(inputs)
    res = run_bass_kernel_spmd(nc, in_maps, core_ids=list(range(8)))
    o = np.empty((B, T, D), np.float32)
    for b in range(B):
        for j in range(2):
            o[b, j * HALF:(j + 1) * HALF] = res.results[b * 2 + j]["out"]
    return o
